# version 07
# baseline (speedup 1.0000x reference)
import numpy as np
from contextlib import ExitStack

import concourse.bass as bass
import concourse.mybir as mybir
from concourse.bass_utils import run_bass_kernel_spmd

F32 = mybir.dt.float32
BF16 = mybir.dt.bfloat16
I32 = mybir.dt.int32
AF = mybir.ActivationFunctionType
ALU = mybir.AluOpType

B, L, D = 4, 4096, 2048
NH, NS, HD = 32, 128, 64
HF = 8192
KT = D // 128
CH = 512
NCH = 2048 // CH
KG = 1024
NGRP = HF // KG
GMT = KG // 128
EPS1 = 1e-5
EPS2 = 1e-8

import os
PIPE_PREFIX = os.environ.get("KPIPE_PREFIX", "1") == "1"
PIPE_SCANGATE = os.environ.get("KPIPE_SCANGATE", "1") == "1"
PIPE_WOQUANT = os.environ.get("KPIPE_WOQUANT", "1") == "1"
PIPE_FFN = os.environ.get("KPIPE_FFN", "1") == "1"

ENGINES = ("sync", "gpsimd", "scalar", "vector", "tensor")
SEM_ENG = {"dS": "sync", "dG": "gpsimd", "dC": "scalar",
           "dR0": "gpsimd", "dR1": "gpsimd", "dY0": "gpsimd", "dY1": "gpsimd",
           "dQ": "gpsimd", "dW0": "gpsimd", "dW1": "gpsimd",
           "dI0": "sync", "dI1": "sync", "dX": "sync",
           "dF0": "sync", "dF1": "sync", "dU0": "sync", "dU1": "sync",
           "dO0": "scalar", "dO1": "scalar", "dC0": "scalar", "dC1": "scalar",
           "V": "vector", "S": "scalar", "T": "tensor", "G": "gpsimd"}


def build_nc():
    nc = bass.Bass("TRN2", target_bir_lowering=False, detect_race_conditions=False)

    hd = {}
    hd["xpre"] = nc.dram_tensor("xpre", [D, 2048], F32, kind="ExternalInput").ap()
    hd["xown"] = nc.dram_tensor("xown", [D, 2048], F32, kind="ExternalInput").ap()
    hd["smask"] = nc.dram_tensor("smask", [128, 1], F32, kind="ExternalInput").ap()
    hd["wx"] = nc.dram_tensor("wx", [D, 288], F32, kind="ExternalInput").ap()
    hd["wg"] = nc.dram_tensor("wg", [D, D], F32, kind="ExternalInput").ap()
    hd["wo"] = nc.dram_tensor("wo", [D, D], F32, kind="ExternalInput").ap()
    hd["wgt"] = nc.dram_tensor("wgt", [D, HF], F32, kind="ExternalInput").ap()
    hd["wut"] = nc.dram_tensor("wut", [D, HF], F32, kind="ExternalInput").ap()
    hd["wdt"] = nc.dram_tensor("wdt", [HF, D], F32, kind="ExternalInput").ap()
    hd["a_t"] = nc.dram_tensor("a_t", [NS, NH], F32, kind="ExternalInput").ap()
    hd["dp_t"] = nc.dram_tensor("dp_t", [128, KT], F32, kind="ExternalInput").ap()
    hd["bg_t"] = nc.dram_tensor("bg_t", [128, KT], F32, kind="ExternalInput").ap()
    hd["out_t"] = nc.dram_tensor("out_t", [D, 2048], F32, kind="ExternalOutput").ap()
    hd["sg_s"] = nc.dram_tensor("sg_s", [HF, D], BF16, kind="Internal").ap()
    hd["su_s"] = nc.dram_tensor("su_s", [HF, D], BF16, kind="Internal").ap()
    hd["sd_s"] = nc.dram_tensor("sd_s", [D, HF], BF16, kind="Internal").ap()

    with ExitStack() as ctx:
        e = ctx.enter_context
        hd["big1"] = e(nc.sbuf_tensor("big1", [128, KT * CH], F32))
        hd["big2"] = e(nc.sbuf_tensor("big2", [128, KT * CH], F32))
        hd["xq"] = e(nc.sbuf_tensor("xq", [128, KT * CH], BF16))
        hd["scrA"] = e(nc.sbuf_tensor("scrA", [128, 2 * GMT * CH], F32))
        hd["p_hi"] = e(nc.sbuf_tensor("p_hi", [128, GMT * CH], BF16))
        hd["p_lo"] = e(nc.sbuf_tensor("p_lo", [128, GMT * CH], BF16))
        hd["wx_r"] = e(nc.sbuf_tensor("wx_r", [128, KT * 288], F32))
        hd["wfp"] = e(nc.sbuf_tensor("wfp", [128, 2 * KT * 128], F32))
        hd["wgu"] = e(nc.sbuf_tensor("wgu", [128, 4 * KT * 128], BF16))
        hd["wdd"] = e(nc.sbuf_tensor("wdd", [128, 2 * GMT * 128], BF16))
        hd["mtl"] = e(nc.sbuf_tensor("mtl", [128, CH], F32))
        hd["mxc"] = e(nc.sbuf_tensor("mxc", [128, CH], F32))
        hd["gt2"] = e(nc.sbuf_tensor("gt2", [128, CH], F32))
        hd["i32t"] = e(nc.sbuf_tensor("i32t", [128, CH], I32))
        hd["y_own"] = e(nc.sbuf_tensor("y_own", [32, 2048], F32))
        hd["eh"] = e(nc.sbuf_tensor("eh", [32, D], F32))
        hd["ident"] = e(nc.sbuf_tensor("ident", [128, 128], F32))
        hd["a_r"] = e(nc.sbuf_tensor("a_r", [NS, NH], F32))
        hd["dp_r"] = e(nc.sbuf_tensor("dp_r", [128, KT], F32))
        hd["bg_r"] = e(nc.sbuf_tensor("bg_r", [128, KT], F32))
        hd["msk"] = e(nc.sbuf_tensor("msk", [128, 1], F32))
        hd["st"] = e(nc.sbuf_tensor("st", [128, NH], F32))
        hd["onc"] = e(nc.sbuf_tensor("onc", [128, 1], F32))
        hd["onr"] = e(nc.sbuf_tensor("onr", [1, 128], F32))
        hd["prow"] = e(nc.psum_tensor("prow", [1, 2 * CH], F32))
        hd["pb"] = [e(nc.psum_tensor(f"pb{i}", [128, CH], F32)) for i in range(6)]
        sems = {k: e(nc.semaphore(k)) for k in SEM_ENG}
        block = e(nc.Block())

        for role in ENGINES:
            def body(eng, role=role):
                walk(nc, eng, role, hd, sems)
            getattr(block, role)(body)
    return nc


def drive(*gens, bg=None, bg_n=1):
    live = list(gens)
    while live:
        for g in list(live):
            try:
                next(g)
            except StopIteration:
                live.remove(g)
        for _ in range(bg_n if bg is not None else 0):
            try:
                next(bg)
            except StopIteration:
                bg = None
                break


def walk(nc, eng, role, hd, sems):
    cnt = dict.fromkeys(SEM_ENG, 0)

    def op(engine, fn, inc=None):
        inst = fn(eng) if engine == role else None
        if inc is not None:
            step = 16 if inc.startswith("d") else 1
            cnt[inc] += step
            if engine == role:
                inst.then_inc(sems[inc], step)
        return inst

    def wait(engine, key, val=None):
        if engine == role:
            eng.wait_ge(sems[key], cnt[key] if val is None else val)

    def dma(queue, dst, src):
        qkey = {"sync": "dS", "gpsimd": "dG", "scalar": "dC"}[queue]
        op(queue, lambda E: E.dma_start(dst, src), inc=qkey)

    def dma_k(queue, key, dst, src):
        op(queue, lambda E: E.dma_start(dst, src), inc=key)

    big1, big2, xqb = hd["big1"], hd["big2"], hd["xq"]
    scrA, p_hi, p_lo = hd["scrA"], hd["p_hi"], hd["p_lo"]
    wx_r, wfp, wgu, wdd = hd["wx_r"], hd["wfp"], hd["wgu"], hd["wdd"]
    mtl, mxc, gt2, i32t = hd["mtl"], hd["mxc"], hd["gt2"], hd["i32t"]
    y_own, ehb, ident = hd["y_own"], hd["eh"], hd["ident"]
    a_r, dp_r, bg_r, msk, st = hd["a_r"], hd["dp_r"], hd["bg_r"], hd["msk"], hd["st"]
    onc, onr, prow, pb = hd["onc"], hd["onr"], hd["prow"], hd["pb"]

    delta = lambda p: scrA[:, p * CH:(p + 1) * CH]
    bsq = lambda p: scrA[:, (2 + p) * CH:(3 + p) * CH]
    hst = lambda p: scrA[:, (4 + p) * CH:(5 + p) * CH]
    sqv = lambda p: scrA[:, (6 + p) * CH:(7 + p) * CH]

    def ssm_bufs(par):
        if par == 0:
            return (scrA[:, 8 * CH:9 * CH], scrA[:, 9 * CH:10 * CH],
                    scrA[0:32, 10 * CH:11 * CH])
        return (mtl[:, :], mxc[:, :], gt2[0:32, :])

    def row(i):
        slot = 11 + i if i < 5 else i - 5
        return scrA[0:1, slot * CH:(slot + 1) * CH]

    dma_k("sync", "dX", a_r[:], hd["a_t"])
    dma_k("sync", "dX", dp_r[:], hd["dp_t"])
    dma_k("sync", "dX", bg_r[:], hd["bg_t"])
    dma_k("sync", "dX", msk[:], hd["smask"])
    for k in range(KT):
        dma_k("sync", "dX", wx_r[:, k * 288:(k + 1) * 288],
              hd["wx"][k * 128:(k + 1) * 128, :])
    op("vector", lambda E: E.memset(onc[:], 1.0))
    op("vector", lambda E: E.memset(onr[:], 1.0))
    op("vector", lambda E: E.memset(st[:, :], 0.0), inc="V")
    op("gpsimd", lambda E: E.memset(ident[:], 0.0), inc="G")
    wait("gpsimd", "G", cnt["G"])
    op("gpsimd", lambda E: E.affine_select(
        out=ident[:], in_=ident[:], compare_op=ALU.not_equal, fill=1.0, base=0,
        pattern=[[-1, 128]], channel_multiplier=1), inc="G")
    op("gpsimd", lambda E: E.memset(ehb[:], 1.0), inc="G")
    wait("gpsimd", "G", cnt["G"])
    op("gpsimd", lambda E: E.affine_select(
        out=ehb[:], in_=ehb[:], compare_op=ALU.is_ge, fill=0.0, base=0,
        pattern=[[1, D]], channel_multiplier=-HD), inc="G")
    wait("gpsimd", "G", cnt["G"])
    op("gpsimd", lambda E: E.affine_select(
        out=ehb[:], in_=ehb[:], compare_op=ALU.is_ge, fill=0.0, base=HD - 1,
        pattern=[[-1, D]], channel_multiplier=HD), inc="G")
    g0_setup = cnt["G"]

    def sign_gen():
        jobs = []
        for (src, dst, rows, cols, dcols) in (
                (hd["wgt"], hd["sg_s"], D, HF, D),
                (hd["wut"], hd["su_s"], D, HF, D),
                (hd["wdt"], hd["sd_s"], HF, D, HF)):
            for r0 in range(0, rows, 128):
                for c0 in range(0, cols, 2048):
                    jobs.append((src, dst, r0, c0, dcols))
        obufs = [p_hi[:, 0:2048], p_hi[:, 2048:4096]]
        for idx, (src, dst, r0, c0, dcols) in enumerate(jobs):
            par = idx % 2
            dI = "dI0" if par == 0 else "dI1"
            dO = "dO0" if par == 0 else "dO1"
            ibuf = big2[:, par * 2048:(par + 1) * 2048]
            obuf = obufs[par]
            if idx >= 2:
                wait("sync", "S", sign_S_hist[idx - 2])
            dma_k("sync", dI, ibuf, src[r0:r0 + 128, c0:c0 + 2048])
            wait("scalar", dI, cnt[dI])
            if idx >= 2:
                wait("scalar", dO, cnt[dO])
            op("scalar", lambda E, i=ibuf, o=obuf: E.sign(o, i), inc="S")
            sign_S_hist.append(cnt["S"])
            wait("scalar", "S", cnt["S"])
            op("scalar", lambda E: E.drain())
            k = r0 // 128
            m0 = c0 // 128
            out_ap = bass.AP(dst.tensor, (m0 * 128) * dcols + k * 128,
                             [[dcols, 128], [128 * dcols, 16], [1, 128]])
            dma_k("scalar", dO, out_ap, obuf)
            yield

    sign_S_hist = []

    def layernorm_chunk(eps, ring2=True):
        wait("tensor", "V", cnt["V"])
        wait("scalar", "dC", cnt["dC"])
        sq_T = []
        nring = 2 if ring2 else 4
        for k in range(KT):
            xk = big1[:, k * CH:(k + 1) * CH]
            sq = (sqv(k % 2) if ring2
                  else [sqv(0), sqv(1), delta(0), delta(1)][k % 4])
            if k >= nring:
                wait("scalar", "T", sq_T[k - nring])
            op("scalar", lambda E, a=sq, b=xk: E.activation(a, b, AF.Square), inc="S")
            op("tensor", lambda E, b=xk, k=k: E.matmul(
                prow[0:1, 0:CH], onc[:], b, start=(k == 0), stop=(k == KT - 1)))
            wait("tensor", "S", cnt["S"])
            op("tensor", lambda E, a=sq, k=k: E.matmul(
                prow[0:1, CH:2 * CH], onc[:], a, start=(k == 0), stop=(k == KT - 1)),
                inc="T")
            sq_T.append(cnt["T"])
        wait("vector", "T", cnt["T"])
        op("vector", lambda E: E.tensor_scalar(row(0), prow[0:1, 0:CH], 1.0 / D, None, ALU.mult))
        op("vector", lambda E: E.tensor_scalar(row(1), prow[0:1, CH:2 * CH], 1.0 / D, None, ALU.mult))
        op("vector", lambda E: E.tensor_tensor(row(2), row(0), row(0), ALU.mult))
        op("vector", lambda E: E.tensor_tensor(row(1), row(1), row(2), ALU.subtract))
        op("vector", lambda E: E.tensor_scalar(row(1), row(1), eps, None, ALU.add))
        op("vector", lambda E: E.reciprocal(row(2), row(1)), inc="V")
        wait("scalar", "V", cnt["V"])
        op("scalar", lambda E: E.activation(row(1), row(2), AF.Sqrt), inc="S")
        wait("vector", "S", cnt["S"])
        op("vector", lambda E: E.tensor_tensor(row(2), row(0), row(1), ALU.mult), inc="V")
        wait("tensor", "V", cnt["V"])
        op("tensor", lambda E: E.matmul(pb[4][:], onr[:], row(1), start=True, stop=True))
        op("tensor", lambda E: E.matmul(pb[5][:], onr[:], row(2), start=True, stop=True), inc="T")
        wait("vector", "T", cnt["T"])
        for k in range(KT):
            xk = big1[:, k * CH:(k + 1) * CH]
            op("vector", lambda E, a=xk: E.tensor_tensor(a, a, pb[4][:], ALU.mult))
            op("vector", lambda E, a=xk: E.tensor_tensor(a, a, pb[5][:], ALU.subtract))
        op("vector", lambda E: E.tensor_copy(row(0), row(0)), inc="V")

    def ssm_chunk(par):
        btv, ctv, dbv = ssm_bufs(par)
        wait("tensor", "V", cnt["V"])
        wait("vector", "dR0", cnt["dR0"])
        wait("vector", "dR1", cnt["dR1"])
        specs = ((0, 128, pb[0], btv), (128, 256, pb[1], ctv), (256, 288, pb[0], dbv))
        for gi, (m0, m1, pdst, sdst) in enumerate(specs):
            mw = m1 - m0
            if gi == 2:
                wait("tensor", "V", cnt["V"])
            for k in range(KT):
                stat = wx_r[:, k * 288 + m0:k * 288 + m1]
                mov = big1[:, k * CH:(k + 1) * CH]
                op("tensor", lambda E, s=stat, m=mov, k=k, pd=pdst, mw=mw: E.matmul(
                    pd[0:mw, :], s, m, start=(k == 0), stop=(k == KT - 1)),
                    inc=("T" if k == KT - 1 else None))
            wait("vector", "T", cnt["T"])
            op("vector", lambda E, pd=pdst, sd=sdst, mw=mw: E.tensor_copy(sd, pd[0:mw, :]),
               inc="V")

    def scan_gen(want_y, yoff, sbpar, dbrows=(2, 3), ystrows=(1, 4)):
        btv, ctv, dbv = ssm_bufs(sbpar)
        zadd_v, dbmm_T, ymm_T = [], [], []
        for h in range(NH):
            par = h % 2
            dl, bl, hl = delta(par), bsq(par), hst(par)
            pdb = pb[2 + par]
            pyy = prow[0:1, par * CH:(par + 1) * CH]
            dbr = row(dbrows[par])
            if h >= 2:
                wait("gpsimd", "T", dbmm_T[h - 2])
            else:
                wait("gpsimd", "V", cnt["V"])
            dR = "dR0" if par == 0 else "dR1"
            op("gpsimd", lambda E, d=dbr, h=h: E.dma_start(d, dbv[h:h + 1, :]), inc=dR)
            if h >= 2:
                wait("tensor", "V", zadd_v[h - 2])
            else:
                wait("tensor", "V", cnt["V"])
            wait("tensor", dR, cnt[dR])
            op("tensor", lambda E, p=pdb, d=dbr: E.matmul(
                p[:], onr[:], d, start=True, stop=True), inc="T")
            dbmm_T.append(cnt["T"])
            wait("vector", "T", cnt["T"])
            op("vector", lambda E, d=dl, p=pdb, b=btv: E.tensor_tensor(d, p[:], b, ALU.add),
               inc="V")
            zadd_v.append(cnt["V"])
            wait("scalar", "V", cnt["V"])
            op("scalar", lambda E, d=dl: E.activation(d, d, AF.Exp))
            op("scalar", lambda E, d=dl: E.activation(d, d, AF.Ln, bias=1.0), inc="S")
            wait("vector", "S", cnt["S"])
            op("vector", lambda E, b=bl, d=dl: E.tensor_tensor(b, d, d, ALU.mult), inc="V")
            wait("scalar", "V", cnt["V"])
            op("scalar", lambda E, d=dl, h=h: E.activation(
                d, d, AF.Exp, scale=a_r[:, h:h + 1]), inc="S")
            wait("vector", "S", cnt["S"])
            op("vector", lambda E, hh=hl, a=dl, b=bl, h=h: E.tensor_tensor_scan(
                hh, a, b, st[:, h:h + 1], ALU.mult, ALU.add))
            op("vector", lambda E: E.drain())
            op("vector", lambda E, hh=hl, h=h: E.tensor_copy(st[:, h:h + 1], hh[:, CH - 1:CH]),
               inc="V")
            if want_y:
                ystg = row(ystrows[par])
                op("vector", lambda E, t=bl, hh=hl, c=ctv: E.tensor_tensor(t, hh, c, ALU.mult),
                   inc="V")
                wait("tensor", "V", cnt["V"])
                op("tensor", lambda E, p=pyy, t=bl: E.matmul(
                    p, onc[:], t, start=True, stop=True), inc="T")
                dY = "dY0" if par == 0 else "dY1"
                wait("vector", "T", cnt["T"])
                if h >= 2:
                    wait("vector", dY, ymm_T[h - 2][1])
                op("vector", lambda E, p=pyy, s=ystg: E.tensor_copy(s, p), inc="V")
                wait("gpsimd", "V", cnt["V"])
                op("gpsimd", lambda E, s=ystg, h=h, yoff=yoff: E.dma_start(
                    y_own[h:h + 1, yoff:yoff + CH], s), inc=dY)
                ymm_T.append((dY, cnt[dY]))
            yield
        op("vector", lambda E: E.tensor_copy(row(0), row(0)), inc="V")

    def gate_gen(j):
        gate_T = []
        for m in range(KT):
            par = m % 2
            wo_off = par * KT * 128
            pg = pb[par]
            if m >= 2:
                wait("sync", "T", gate_T[m - 2])
            else:
                wait("sync", "T", cnt["T"])
            dF = "dF0" if par == 0 else "dF1"
            dma_k("sync", dF, wfp[:, wo_off:wo_off + KT * 128],
                  hd["wg"][m * 128:(m + 1) * 128, :])
            wait("tensor", dF, cnt[dF])
            if m >= 2:
                wait("tensor", "S", sigm_S[m - 2])
            else:
                wait("tensor", "V", cnt["V"])
            for k in range(KT):
                op("tensor", lambda E, o=wo_off, k=k, p=pg: E.matmul(
                    p[:], wfp[:, o + k * 128:o + (k + 1) * 128],
                    big1[:, k * CH:(k + 1) * CH],
                    start=(k == 0), stop=(k == KT - 1)), inc=("T" if k == KT - 1 else None))
            gate_T.append(cnt["T"])
            wait("scalar", "T", cnt["T"])
            op("scalar", lambda E, p=pg, m=m: E.activation(
                big2[:, m * CH:(m + 1) * CH], p[:], AF.Sigmoid, bias=bg_r[:, m:m + 1]),
                inc="S")
            sigm_S.append(cnt["S"])
            yield

    sigm_S = []

    def quant_gen(j, round_V):
        wait("tensor", "V", cnt["V"])
        wait("scalar", "V", cnt["V"])
        wait("scalar", "dY0", cnt["dY0"])
        wait("scalar", "dY1", cnt["dY1"])
        qsq_T = []
        for k in range(KT):
            xk = big1[:, k * CH:(k + 1) * CH]
            sq = [sqv(0), sqv(1), delta(0), delta(1)][k % 4]
            if k >= 4:
                wait("scalar", "T", qsq_T[k - 4])
            op("scalar", lambda E, a=sq, b=xk: E.activation(a, b, AF.Square), inc="S")
            op("tensor", lambda E, b=xk, k=k: E.matmul(
                prow[0:1, 0:CH], onc[:], b, start=(k == 0), stop=(k == KT - 1)))
            wait("tensor", "S", cnt["S"])
            op("tensor", lambda E, a=sq, k=k: E.matmul(
                prow[0:1, CH:2 * CH], onc[:], a, start=(k == 0), stop=(k == KT - 1)),
                inc="T")
            qsq_T.append(cnt["T"])
            if k % 4 == 3:
                yield
        wait("vector", "T", cnt["T"])
        op("vector", lambda E: E.tensor_scalar(row(0), prow[0:1, 0:CH], 1.0 / D, None, ALU.mult))
        op("vector", lambda E: E.tensor_scalar(row(1), prow[0:1, CH:2 * CH], 1.0 / D, None, ALU.mult))
        op("vector", lambda E: E.tensor_tensor(row(2), row(0), row(0), ALU.mult))
        op("vector", lambda E: E.tensor_tensor(row(1), row(1), row(2), ALU.subtract))
        op("vector", lambda E: E.tensor_scalar(row(1), row(1), EPS2, None, ALU.add))
        op("vector", lambda E: E.reciprocal(row(2), row(1)), inc="V")
        wait("scalar", "V", cnt["V"])
        op("scalar", lambda E: E.activation(row(1), row(2), AF.Sqrt), inc="S")
        wait("tensor", "S", cnt["S"])
        op("tensor", lambda E: E.matmul(pb[4][:], onr[:], row(0), start=True, stop=True),
           inc="T")
        wait("vector", "T", cnt["T"])
        yield
        for k in range(KT):
            xk = big1[:, k * CH:(k + 1) * CH]
            sq = sqv(0)
            op("vector", lambda E, a=sq, b=xk: E.tensor_tensor(a, b, pb[4][:], ALU.subtract))
            if k == 0:
                op("vector", lambda E, a=sq: E.tensor_copy(mtl[:, :], a))
                op("vector", lambda E, a=sq: E.tensor_copy(mxc[:, :], a))
            else:
                op("vector", lambda E, a=sq: E.tensor_tensor(mtl[:, :], mtl[:, :], a, ALU.max))
                op("vector", lambda E, a=sq: E.tensor_tensor(mxc[:, :], mxc[:, :], a, ALU.min))
            if k % 4 == 3:
                yield
        op("vector", lambda E: E.tensor_scalar(mxc[:, :], mxc[:, :], -1.0, None, ALU.mult))
        op("vector", lambda E: E.tensor_tensor(mtl[:, :], mtl[:, :], mxc[:, :], ALU.max))
        op("vector", lambda E: E.tensor_copy(row(0), row(0)), inc="V")
        wait("tensor", "V", cnt["V"])
        wait("tensor", "G", g0_setup)
        for q in range(4):
            op("tensor", lambda E, q=q: E.transpose(
                pb[5][:, q * 128:(q + 1) * 128], mtl[:, q * 128:(q + 1) * 128], ident[:]),
                inc=("T" if q == 3 else None))
        wait("vector", "T", cnt["T"])
        for q in range(4):
            op("vector", lambda E, q=q: E.reduce_max(
                sqv(0)[:, q:q + 1], pb[5][:, q * 128:(q + 1) * 128], mybir.AxisListType.X))
        op("vector", lambda E: E.tensor_copy(row(0)[0:1, 0:1], row(0)[0:1, 0:1]), inc="V")
        wait("gpsimd", "V", cnt["V"])
        sq0 = sqv(0)
        r3 = row(3)
        for q in range(4):
            op("gpsimd", lambda E, q=q: E.dma_start(
                bass.AP(r3.tensor, r3.offset + q * 128, [[r3.ap[0][0], 1], [1, 128]]),
                bass.AP(sq0.tensor, sq0.offset + q, [[sq0.ap[0][0], 128], [1, 1]]),
                allow_slow_non_contiguous=True), inc="dQ")
        wait("vector", "dQ", cnt["dQ"])
        op("vector", lambda E: E.reciprocal(row(2), row(3)))
        op("vector", lambda E: E.tensor_scalar(row(4), row(2), 127.0, None, ALU.mult))
        op("vector", lambda E: E.tensor_tensor(row(5), row(0), row(4), ALU.mult))
        op("vector", lambda E: E.tensor_scalar(row(5), row(5), -1.0, None, ALU.mult))
        op("vector", lambda E: E.tensor_scalar(row(6), row(3), 1.0 / 127.0, None, ALU.mult))
        op("vector", lambda E: E.tensor_tensor(row(6), row(6), row(1), ALU.mult), inc="V")
        wait("tensor", "V", cnt["V"])
        op("tensor", lambda E: E.matmul(pb[4][:], onr[:], row(6), start=True, stop=True), inc="T")
        wait("vector", "T", cnt["T"])
        op("vector", lambda E: E.tensor_copy(mxc[:, :], pb[4][:]), inc="V")
        wait("tensor", "V", cnt["V"])
        op("tensor", lambda E: E.matmul(pb[4][:], onr[:], row(4), start=True, stop=True))
        op("tensor", lambda E: E.matmul(pb[5][:], onr[:], row(5), start=True, stop=True), inc="T")
        wait("vector", "T", cnt["T"])
        yield
        for k in range(KT):
            xk = big1[:, k * CH:(k + 1) * CH]
            vv, dd = sqv(0), sqv(1)
            xqs = xqb[:, k * CH:(k + 1) * CH]
            op("vector", lambda E, v=vv, x=xk: E.tensor_tensor(v, x, pb[4][:], ALU.mult))
            op("vector", lambda E, v=vv: E.tensor_tensor(v, v, pb[5][:], ALU.add))
            op("vector", lambda E, v=vv: E.tensor_copy(i32t[:, :], v))
            op("vector", lambda E, d=dd: E.tensor_copy(d, i32t[:, :]))
            op("vector", lambda E, v=vv, d=dd: E.tensor_tensor(v, v, d, ALU.subtract))
            op("vector", lambda E, v=vv: E.tensor_scalar(mtl[:, :], v, 0.5, None, ALU.is_gt))
            op("vector", lambda E, v=vv: E.tensor_scalar(v, v, -0.5, None, ALU.is_lt))
            op("vector", lambda E, v=vv: E.tensor_tensor(mtl[:, :], mtl[:, :], v, ALU.subtract))
            op("vector", lambda E, d=dd: E.tensor_tensor(d, d, mtl[:, :], ALU.add))
            op("vector", lambda E, x=xqs, d=dd: E.tensor_scalar(
                x, d, -128.0, 127.0, ALU.max, ALU.min))
            round_V.append(cnt["V"])
            if k % 2 == 1:
                yield
        op("vector", lambda E: E.tensor_copy(row(0)[0:1, 0:1], row(0)[0:1, 0:1]), inc="V")

    def yfwo_gen(j, round_V):
        ysl = y_own[:, j * CH:(j + 1) * CH]
        yf_V = []
        for m in range(KT):
            par = m % 2
            pyb = pb[2 + par]
            if m == 0:
                wait("tensor", "dY0", cnt["dY0"])
                wait("tensor", "dY1", cnt["dY1"])
                wait("tensor", "G", g0_setup)
                wait("tensor", "S", cnt["S"])
            if m >= 2:
                wait("tensor", "V", yf_V[m - 2])
            op("tensor", lambda E, m=m, p=pyb: E.matmul(
                p[:], ehb[:, m * 128:(m + 1) * 128], ysl, start=True, stop=True), inc="T")
            wait("vector", "T", cnt["T"])
            xk = big1[:, m * CH:(m + 1) * CH]
            gk = big2[:, m * CH:(m + 1) * CH]
            tv = bsq(0)
            op("vector", lambda E, x=xk, p=pyb, m=m, t=tv: E.scalar_tensor_tensor(
                t, x, dp_r[:, m:m + 1], p[:], ALU.mult, ALU.add))
            op("vector", lambda E, g=gk, t=tv: E.tensor_tensor(g, g, t, ALU.mult), inc="V")
            yf_V.append(cnt["V"])
            if m % 4 == 3:
                yield
        for _ in wo_gen(j, round_V):
            yield

    def wo_gen(j, round_V):
        wo_T = []
        acc_V = []
        for m in range(KT):
            bank = m % 4
            wo_off = (m % 2) * KT * 128
            pg = pb[bank]
            if m >= 2:
                wait("sync", "T", wo_T[m - 2])
            else:
                wait("sync", "V", cnt["V"])
                wait("sync", "T", cnt["T"])
            dF = "dF0" if m % 2 == 0 else "dF1"
            dma_k("sync", dF, wfp[:, wo_off:wo_off + KT * 128],
                  hd["wo"][m * 128:(m + 1) * 128, :])
            wait("tensor", dF, cnt[dF])
            if m >= 4:
                wait("tensor", "V", acc_V[m - 4])
            else:
                wait("tensor", "V", cnt["V"])
            for k in range(KT):
                op("tensor", lambda E, o=wo_off, k=k, p=pg: E.matmul(
                    p[:], wfp[:, o + k * 128:o + (k + 1) * 128],
                    big2[:, k * CH:(k + 1) * CH],
                    start=(k == 0), stop=(k == KT - 1)), inc=("T" if k == KT - 1 else None))
            wo_T.append(cnt["T"])
            while len(round_V) <= m:
                yield
            wait("vector", "T", wo_T[m])
            ok = big1[:, m * CH:(m + 1) * CH]
            op("vector", lambda E, o=ok, p=pg: E.tensor_copy(o, p[:]), inc="V")
            acc_V.append(cnt["V"])
            yield

    def load_x_chunk(src, j):
        wait("sync", "V", cnt["V"])
        wait("sync", "T", cnt["T"])
        wait("sync", "dC", cnt["dC"])
        for k in range(KT):
            dma_k("sync", "dX", big1[:, k * CH:(k + 1) * CH],
                  src[k * 128:(k + 1) * 128, j * CH:(j + 1) * CH])
        wait("scalar", "dX", cnt["dX"])
        wait("tensor", "dX", cnt["dX"])
        wait("vector", "dX", cnt["dX"])

    sgen = sign_gen()

    def prep_gen(j):
        load_x_chunk(hd["xpre"], j)
        yield
        layernorm_chunk(EPS1)
        yield
        ssm_chunk(j % 2)
        yield

    if PIPE_PREFIX:
        drive(prep_gen(0), bg=sgen, bg_n=2)
        for j in range(NCH):
            gens = [scan_gen(False, 0, j % 2, dbrows=(3, 4))]
            if j + 1 < NCH:
                gens.append(prep_gen(j + 1))
            drive(*gens, bg=sgen, bg_n=2)
    else:
        for j in range(NCH):
            drive(prep_gen(j))
            drive(scan_gen(False, 0, j % 2, dbrows=(3, 4)))
    op("vector", lambda E: E.tensor_scalar(st[:, :], st[:, :], msk[:, 0:1], None, ALU.mult),
       inc="V")
    drive(sgen)
    sign_done = (cnt["dO0"], cnt["dO1"])

    for j in range(NCH):
        load_x_chunk(hd["xown"], j)
        layernorm_chunk(EPS1, ring2=False)
        ssm_chunk(0)
        sigm_S.clear()
        if PIPE_SCANGATE:
            drive(scan_gen(True, j * CH, 0), gate_gen(j))
        else:
            drive(scan_gen(True, j * CH, 0))
            drive(gate_gen(j))
        round_V = []
        if PIPE_WOQUANT:
            drive(yfwo_gen(j, round_V), quant_gen(j, round_V))
        else:
            drive(yfwo_gen(j, round_V))
            drive(quant_gen(j, round_V))
        wait("sync", "dO0", sign_done[0])
        wait("sync", "dO1", sign_done[1])
        wait("gpsimd", "dO0", sign_done[0])
        wait("gpsimd", "dO1", sign_done[1])
        wait("vector", "dO0", sign_done[0])
        wait("vector", "dO1", sign_done[1])
        wait("tensor", "V", cnt["V"])
        gu_T = []
        ev_hist = []
        psplit_hist = []
        wd_last_T = []
        accD_hist = []
        xres_k = [0]
        xadd_V = []

        def xres_step():
            k = xres_k[0]
            if k >= KT:
                return
            xres_k[0] += 1
            stg = mtl[:, :] if k % 2 == 0 else gt2[:, :]
            if k >= 2:
                wait("scalar", "V", xadd_V[k - 2])
            else:
                wait("scalar", "V", cnt["V"])
            dCp = "dC0" if k % 2 == 0 else "dC1"
            op("scalar", lambda E, k=k, s=stg: E.dma_start(
                s, hd["xown"][k * 128:(k + 1) * 128, j * CH:(j + 1) * CH]), inc=dCp)
            wait("vector", dCp, cnt[dCp])
            ok = big1[:, k * CH:(k + 1) * CH]
            op("vector", lambda E, o=ok, s=stg: E.tensor_tensor(o, o, s, ALU.add), inc="V")
            xadd_V.append(cnt["V"])

        def gubuf(g):
            return scrA if g % 2 == 0 else big2

        def gu_part(g):
            buf = gubuf(g)
            for mi in range(GMT):
                hm = g * GMT + mi
                par2 = hm % 2
                og = (2 * par2) * KT * 128
                ou = (2 * par2 + 1) * KT * 128
                if len(gu_T) >= 2:
                    wait("sync", "T", gu_T[-2])
                dU = "dU0" if par2 == 0 else "dU1"
                dma_k("sync", dU, wgu[:, og:og + KT * 128],
                      hd["sg_s"][hm * 128:(hm + 1) * 128, :])
                dma_k("sync", dU, wgu[:, ou:ou + KT * 128],
                      hd["su_s"][hm * 128:(hm + 1) * 128, :])
                pgg, puu = pb[mi % 2], pb[2 + mi % 2]
                wait("tensor", dU, cnt[dU])
                if mi >= 2:
                    wait("tensor", "V", ev_hist[-2])
                elif g > 0 and ev_hist:
                    wait("tensor", "V", ev_hist[-1])
                for k in range(KT):
                    op("tensor", lambda E, o=og, k=k, p=pgg: E.matmul(
                        p[:], wgu[:, o + k * 128:o + (k + 1) * 128],
                        xqb[:, k * CH:(k + 1) * CH],
                        start=(k == 0), stop=(k == KT - 1)))
                for k in range(KT):
                    op("tensor", lambda E, o=ou, k=k, p=puu: E.matmul(
                        p[:], wgu[:, o + k * 128:o + (k + 1) * 128],
                        xqb[:, k * CH:(k + 1) * CH],
                        start=(k == 0), stop=(k == KT - 1)),
                        inc=("T" if k == KT - 1 else None))
                gu_T.append(cnt["T"])
                wait("vector", "T", cnt["T"])
                gsl = buf[:, mi * CH:(mi + 1) * CH]
                usl = buf[:, (GMT + mi) * CH:(GMT + mi + 1) * CH]
                op("vector", lambda E, a=gsl, p=pgg: E.tensor_tensor(a, p[:], mxc[:, :], ALU.mult))
                op("vector", lambda E, a=usl, p=puu: E.tensor_copy(a, p[:]), inc="V")
                ev_hist.append(cnt["V"])
                if mi % 2 == 1:
                    xres_step()

        def pchain(g):
            buf = gubuf(g)
            gall = buf[:, 0:GMT * CH]
            uall = buf[:, GMT * CH:2 * GMT * CH]
            wait("scalar", "V", ev_hist[-1])
            op("scalar", lambda E, a=gall: E.activation(a, a, AF.Silu), inc="S")
            wait("vector", "S", cnt["S"])
            if len(wd_last_T) >= 1:
                wait("vector", "T", wd_last_T[-1])
            op("vector", lambda E, a=gall, b=uall: E.tensor_tensor(a, a, b, ALU.mult))
            op("vector", lambda E, a=gall: E.tensor_tensor(
                a.rearrange("p (t c) -> p t c", c=CH),
                a.rearrange("p (t c) -> p t c", c=CH),
                mxc[:, :].unsqueeze(1).broadcast_to([128, GMT, CH]), ALU.mult))
            op("vector", lambda E, a=gall: E.tensor_copy(p_hi[:, :], a))
            op("vector", lambda E, a=gall: E.tensor_tensor(p_lo[:, :], a, p_hi[:, :], ALU.subtract),
               inc="V")
            psplit_hist.append(cnt["V"])

        def wd_part(g):
            wd_T = []
            for m in range(KT):
                par3 = m % 2
                od = par3 * GMT * 128
                if m >= 2:
                    wait("gpsimd", "T", wd_T[m - 2])
                else:
                    wait("gpsimd", "T", cnt["T"])
                dma_k("gpsimd", "dW0" if par3 == 0 else "dW1",
                      wdd[:, od:od + GMT * 128],
                      hd["sd_s"][m * 128:(m + 1) * 128, g * KG:(g + 1) * KG])
                pdd = pb[4 + par3]
                wait("tensor", "dW0" if par3 == 0 else "dW1",
                     cnt["dW0" if par3 == 0 else "dW1"])
                if m == 0:
                    wait("tensor", "V", psplit_hist[-1])
                    if len(accD_hist) >= 1:
                        wait("tensor", "V", accD_hist[-1])
                elif m == 1 and len(accD_hist) >= 1:
                    wait("tensor", "V", accD_hist[-1])
                elif m >= 2:
                    wait("tensor", "V", accD_hist[-(KT - m + 2)] if False else accD_now[m - 2])
                for kk in range(GMT):
                    op("tensor", lambda E, o=od, kk=kk, p=pdd: E.matmul(
                        p[:], wdd[:, o + kk * 128:o + (kk + 1) * 128],
                        p_hi[:, kk * CH:(kk + 1) * CH],
                        start=(kk == 0), stop=False))
                for kk in range(GMT):
                    op("tensor", lambda E, o=od, kk=kk, p=pdd: E.matmul(
                        p[:], wdd[:, o + kk * 128:o + (kk + 1) * 128],
                        p_lo[:, kk * CH:(kk + 1) * CH],
                        start=False, stop=(kk == GMT - 1)),
                        inc=("T" if kk == GMT - 1 else None))
                wd_T.append(cnt["T"])
                wait("vector", "T", cnt["T"])
                ok = big1[:, m * CH:(m + 1) * CH]
                op("vector", lambda E, o=ok, p=pdd: E.tensor_tensor(o, o, p[:], ALU.add), inc="V")
                accD_now.append(cnt["V"])
            wd_last_T.append(wd_T[-1])
            accD_hist.append(accD_now[-1])

        if PIPE_FFN:
            gu_part(0)
            for g in range(1, NGRP):
                pchain(g - 1)
                accD_now = []
                gu_part(g)
                wd_part(g - 1)
            pchain(NGRP - 1)
            accD_now = []
            wd_part(NGRP - 1)
        else:
            for g in range(NGRP):
                gu_part(g)
                pchain(g)
                accD_now = []
                wd_part(g)
        while xres_k[0] < KT:
            xres_step()
        wait("scalar", "V", cnt["V"])
        for k in range(KT):
            dma("scalar", hd["out_t"][k * 128:(k + 1) * 128, j * CH:(j + 1) * CH],
                big1[:, k * CH:(k + 1) * CH])

    wait("gpsimd", "dC", cnt["dC"])
    op("gpsimd", lambda E: E.memset(st[0:1, 0:1], 0.0), inc="G")


_NC_CACHE = {}


def kernel(L0_attn_x, L0_attn_Wx, L0_attn_Wg, L0_attn_bg, L0_attn_Wo,
           L0_attn_A, L0_attn_Dp, L0_ffn_W_g, L0_ffn_W_u, L0_ffn_W_d):
    x = np.ascontiguousarray(np.asarray(L0_attn_x, dtype=np.float32))
    wx = np.ascontiguousarray(np.asarray(L0_attn_Wx, dtype=np.float32))
    wg = np.ascontiguousarray(np.asarray(L0_attn_Wg, dtype=np.float32))
    bg = np.ascontiguousarray(np.asarray(L0_attn_bg, dtype=np.float32))
    wo = np.ascontiguousarray(np.asarray(L0_attn_Wo, dtype=np.float32))
    a = np.ascontiguousarray(np.asarray(L0_attn_A, dtype=np.float32))
    dp = np.ascontiguousarray(np.asarray(L0_attn_Dp, dtype=np.float32))
    w_g = np.ascontiguousarray(np.asarray(L0_ffn_W_g, dtype=np.float32))
    w_u = np.ascontiguousarray(np.asarray(L0_ffn_W_u, dtype=np.float32))
    w_d = np.ascontiguousarray(np.asarray(L0_ffn_W_d, dtype=np.float32))

    xT = np.ascontiguousarray(x.transpose(0, 2, 1))
    wg_st = np.ascontiguousarray(
        wg.reshape(KT, 128, KT, 128).transpose(2, 1, 0, 3).reshape(D, D))
    wo_st = np.ascontiguousarray(
        wo.reshape(KT, 128, KT, 128).transpose(2, 1, 0, 3).reshape(D, D))
    wgtT = np.ascontiguousarray(w_g.T)
    wutT = np.ascontiguousarray(w_u.T)
    wdtT = np.ascontiguousarray(w_d.T)
    a_tT = np.ascontiguousarray(a.T)
    dp_til = np.ascontiguousarray(np.repeat(dp, HD).reshape(KT, 128).T)
    bg_til = np.ascontiguousarray(bg.reshape(KT, 128).T)
    zeros_pre = np.zeros((D, 2048), dtype=np.float32)

    if "nc" not in _NC_CACHE:
        _NC_CACHE["nc"] = build_nc()
    nc = _NC_CACHE["nc"]

    in_maps = []
    for c in range(8):
        b, half = c // 2, c % 2
        xpre = zeros_pre if half == 0 else np.ascontiguousarray(xT[b, :, 0:2048])
        xown = np.ascontiguousarray(xT[b, :, half * 2048:(half + 1) * 2048])
        m = np.full((128, 1), 0.0 if half == 0 else 1.0, dtype=np.float32)
        in_maps.append({
            "xpre": xpre, "xown": xown, "smask": m,
            "wx": wx, "wg": wg_st, "wo": wo_st,
            "wgt": wgtT, "wut": wutT, "wdt": wdtT,
            "a_t": a_tT, "dp_t": dp_til, "bg_t": bg_til,
        })
    res = run_bass_kernel_spmd(nc, in_maps, core_ids=list(range(8)))
    out = np.empty((B, L, D), dtype=np.float32)
    for c in range(8):
        b, half = c // 2, c % 2
        out[b, half * 2048:(half + 1) * 2048, :] = res.results[c]["out_t"].T
    return out


# version 08
# speedup vs baseline: 1.0169x; 1.0169x over previous
import numpy as np
from contextlib import ExitStack

import concourse.bass as bass
import concourse.mybir as mybir
from concourse.bass_utils import run_bass_kernel_spmd

F32 = mybir.dt.float32
BF16 = mybir.dt.bfloat16
I32 = mybir.dt.int32
AF = mybir.ActivationFunctionType
ALU = mybir.AluOpType

B, L, D = 4, 4096, 2048
NH, NS, HD = 32, 128, 64
HF = 8192
KT = D // 128
CH = 512
NCH = 2048 // CH
KG = 1024
NGRP = HF // KG
GMT = KG // 128
EPS1 = 1e-5
EPS2 = 1e-8

import os
PIPE_PREFIX = os.environ.get("KPIPE_PREFIX", "1") == "1"
PIPE_SCANGATE = os.environ.get("KPIPE_SCANGATE", "1") == "1"
PIPE_WOQUANT = os.environ.get("KPIPE_WOQUANT", "1") == "1"
PIPE_FFN = os.environ.get("KPIPE_FFN", "1") == "1"

ENGINES = ("sync", "gpsimd", "scalar", "vector", "tensor")
SEM_ENG = {"dS": "sync", "dG": "gpsimd", "dC": "scalar",
           "dR0": "gpsimd", "dR1": "gpsimd", "dY0": "gpsimd", "dY1": "gpsimd",
           "dQ": "gpsimd", "dW0": "gpsimd", "dW1": "gpsimd",
           "dI0": "sync", "dI1": "sync", "dX": "sync",
           "dF0": "sync", "dF1": "sync", "dU0": "sync", "dU1": "sync",
           "dO0": "scalar", "dO1": "scalar", "dC0": "scalar", "dC1": "scalar",
           "V": "vector", "S": "scalar", "T": "tensor", "G": "gpsimd"}


def build_nc():
    nc = bass.Bass("TRN2", target_bir_lowering=False, detect_race_conditions=False)

    hd = {}
    hd["xpre"] = nc.dram_tensor("xpre", [D, 2048], F32, kind="ExternalInput").ap()
    hd["xown"] = nc.dram_tensor("xown", [D, 2048], F32, kind="ExternalInput").ap()
    hd["smask"] = nc.dram_tensor("smask", [128, 1], F32, kind="ExternalInput").ap()
    hd["wx"] = nc.dram_tensor("wx", [D, 288], F32, kind="ExternalInput").ap()
    hd["wg"] = nc.dram_tensor("wg", [D, D], F32, kind="ExternalInput").ap()
    hd["wo"] = nc.dram_tensor("wo", [D, D], F32, kind="ExternalInput").ap()
    hd["wgt"] = nc.dram_tensor("wgt", [D, HF], F32, kind="ExternalInput").ap()
    hd["wut"] = nc.dram_tensor("wut", [D, HF], F32, kind="ExternalInput").ap()
    hd["wdt"] = nc.dram_tensor("wdt", [HF, D], F32, kind="ExternalInput").ap()
    hd["a_t"] = nc.dram_tensor("a_t", [NS, NH], F32, kind="ExternalInput").ap()
    hd["dp_t"] = nc.dram_tensor("dp_t", [128, KT], F32, kind="ExternalInput").ap()
    hd["bg_t"] = nc.dram_tensor("bg_t", [128, KT], F32, kind="ExternalInput").ap()
    hd["out_t"] = nc.dram_tensor("out_t", [D, 2048], F32, kind="ExternalOutput").ap()
    hd["sg_s"] = nc.dram_tensor("sg_s", [HF, D], BF16, kind="Internal").ap()
    hd["su_s"] = nc.dram_tensor("su_s", [HF, D], BF16, kind="Internal").ap()
    hd["sd_s"] = nc.dram_tensor("sd_s", [D, HF], BF16, kind="Internal").ap()

    with ExitStack() as ctx:
        e = ctx.enter_context
        hd["big1"] = e(nc.sbuf_tensor("big1", [128, KT * CH], F32))
        hd["big2"] = e(nc.sbuf_tensor("big2", [128, KT * CH], F32))
        hd["xq"] = e(nc.sbuf_tensor("xq", [128, KT * CH], BF16))
        hd["scrA"] = e(nc.sbuf_tensor("scrA", [128, 2 * GMT * CH], F32))
        hd["p_hi"] = e(nc.sbuf_tensor("p_hi", [128, GMT * CH], BF16))
        hd["p_lo"] = e(nc.sbuf_tensor("p_lo", [128, GMT * CH], BF16))
        hd["wx_r"] = e(nc.sbuf_tensor("wx_r", [128, KT * 288], F32))
        hd["wfp"] = e(nc.sbuf_tensor("wfp", [128, 2 * KT * 128], F32))
        hd["wgu"] = e(nc.sbuf_tensor("wgu", [128, 4 * KT * 128], BF16))
        hd["wdd"] = e(nc.sbuf_tensor("wdd", [128, 2 * GMT * 128], BF16))
        hd["mtl"] = e(nc.sbuf_tensor("mtl", [128, CH], F32))
        hd["mxc"] = e(nc.sbuf_tensor("mxc", [128, CH], F32))
        hd["gt2"] = e(nc.sbuf_tensor("gt2", [128, CH], F32))
        hd["i32t"] = e(nc.sbuf_tensor("i32t", [128, CH], I32))
        hd["y_own"] = e(nc.sbuf_tensor("y_own", [32, 2048], F32))
        hd["eh"] = e(nc.sbuf_tensor("eh", [32, D], F32))
        hd["ident"] = e(nc.sbuf_tensor("ident", [128, 128], F32))
        hd["a_r"] = e(nc.sbuf_tensor("a_r", [NS, NH], F32))
        hd["dp_r"] = e(nc.sbuf_tensor("dp_r", [128, KT], F32))
        hd["bg_r"] = e(nc.sbuf_tensor("bg_r", [128, KT], F32))
        hd["msk"] = e(nc.sbuf_tensor("msk", [128, 1], F32))
        hd["st"] = e(nc.sbuf_tensor("st", [128, NH], F32))
        hd["onc"] = e(nc.sbuf_tensor("onc", [128, 1], F32))
        hd["onr"] = e(nc.sbuf_tensor("onr", [1, 128], F32))
        hd["prow"] = e(nc.psum_tensor("prow", [1, 2 * CH], F32))
        hd["pb"] = [e(nc.psum_tensor(f"pb{i}", [128, CH], F32)) for i in range(6)]
        sems = {k: e(nc.semaphore(k)) for k in SEM_ENG}
        block = e(nc.Block())

        for role in ENGINES:
            def body(eng, role=role):
                walk(nc, eng, role, hd, sems)
            getattr(block, role)(body)
    return nc


def drive(*gens, bg=None, bg_n=1):
    live = list(gens)
    while live:
        for g in list(live):
            try:
                next(g)
            except StopIteration:
                live.remove(g)
        for _ in range(bg_n if bg is not None else 0):
            try:
                next(bg)
            except StopIteration:
                bg = None
                break


def walk(nc, eng, role, hd, sems):
    cnt = dict.fromkeys(SEM_ENG, 0)

    def op(engine, fn, inc=None):
        inst = fn(eng) if engine == role else None
        if inc is not None:
            step = 16 if inc.startswith("d") else 1
            cnt[inc] += step
            if engine == role:
                inst.then_inc(sems[inc], step)
        return inst

    def wait(engine, key, val=None):
        if engine == role:
            eng.wait_ge(sems[key], cnt[key] if val is None else val)

    def dma(queue, dst, src):
        qkey = {"sync": "dS", "gpsimd": "dG", "scalar": "dC"}[queue]
        op(queue, lambda E: E.dma_start(dst, src), inc=qkey)

    def dma_k(queue, key, dst, src):
        op(queue, lambda E: E.dma_start(dst, src), inc=key)

    big1, big2, xqb = hd["big1"], hd["big2"], hd["xq"]
    scrA, p_hi, p_lo = hd["scrA"], hd["p_hi"], hd["p_lo"]
    wx_r, wfp, wgu, wdd = hd["wx_r"], hd["wfp"], hd["wgu"], hd["wdd"]
    mtl, mxc, gt2, i32t = hd["mtl"], hd["mxc"], hd["gt2"], hd["i32t"]
    y_own, ehb, ident = hd["y_own"], hd["eh"], hd["ident"]
    a_r, dp_r, bg_r, msk, st = hd["a_r"], hd["dp_r"], hd["bg_r"], hd["msk"], hd["st"]
    onc, onr, prow, pb = hd["onc"], hd["onr"], hd["prow"], hd["pb"]

    delta = lambda p: scrA[:, p * CH:(p + 1) * CH]
    bsq = lambda p: scrA[:, (2 + p) * CH:(3 + p) * CH]
    hst = lambda p: scrA[:, (4 + p) * CH:(5 + p) * CH]
    sqv = lambda p: scrA[:, (6 + p) * CH:(7 + p) * CH]

    def ssm_bufs(par):
        if par == 0:
            return (scrA[:, 8 * CH:9 * CH], scrA[:, 9 * CH:10 * CH],
                    scrA[0:32, 10 * CH:11 * CH])
        return (mtl[:, :], mxc[:, :], gt2[0:32, :])

    def row(i):
        slot = 11 + i if i < 5 else i - 5
        return scrA[0:1, slot * CH:(slot + 1) * CH]

    dma_k("sync", "dX", a_r[:], hd["a_t"])
    dma_k("sync", "dX", dp_r[:], hd["dp_t"])
    dma_k("sync", "dX", bg_r[:], hd["bg_t"])
    dma_k("sync", "dX", msk[:], hd["smask"])
    for k in range(KT):
        dma_k("sync", "dX", wx_r[:, k * 288:(k + 1) * 288],
              hd["wx"][k * 128:(k + 1) * 128, :])
    op("vector", lambda E: E.memset(onc[:], 1.0))
    op("vector", lambda E: E.memset(onr[:], 1.0))
    op("vector", lambda E: E.memset(st[:, :], 0.0), inc="V")
    op("gpsimd", lambda E: E.memset(ident[:], 0.0), inc="G")
    wait("gpsimd", "G", cnt["G"])
    op("gpsimd", lambda E: E.affine_select(
        out=ident[:], in_=ident[:], compare_op=ALU.not_equal, fill=1.0, base=0,
        pattern=[[-1, 128]], channel_multiplier=1), inc="G")
    op("gpsimd", lambda E: E.memset(ehb[:], 1.0), inc="G")
    wait("gpsimd", "G", cnt["G"])
    op("gpsimd", lambda E: E.affine_select(
        out=ehb[:], in_=ehb[:], compare_op=ALU.is_ge, fill=0.0, base=0,
        pattern=[[1, D]], channel_multiplier=-HD), inc="G")
    wait("gpsimd", "G", cnt["G"])
    op("gpsimd", lambda E: E.affine_select(
        out=ehb[:], in_=ehb[:], compare_op=ALU.is_ge, fill=0.0, base=HD - 1,
        pattern=[[-1, D]], channel_multiplier=HD), inc="G")
    g0_setup = cnt["G"]

    def sign_gen():
        jobs = []
        for (src, dst, rows, cols, dcols) in (
                (hd["wgt"], hd["sg_s"], D, HF, D),
                (hd["wut"], hd["su_s"], D, HF, D),
                (hd["wdt"], hd["sd_s"], HF, D, HF)):
            for r0 in range(0, rows, 128):
                for c0 in range(0, cols, 4096):
                    jobs.append((src, dst, r0, c0, dcols, min(4096, cols - c0)))
        for idx, (src, dst, r0, c0, dcols, w) in enumerate(jobs):
            par = idx % 2
            dI = "dI0" if par == 0 else "dI1"
            dO = "dO0" if par == 0 else "dO1"
            ibuf = big2[:, par * 4096:par * 4096 + w]
            obuf = (p_hi if par == 0 else p_lo)[:, 0:w]
            if idx >= 2:
                wait("sync", "S", sign_S_hist[idx - 2])
            dma_k("sync", dI, ibuf, src[r0:r0 + 128, c0:c0 + w])
            wait("scalar", dI, cnt[dI])
            if idx >= 2:
                wait("scalar", dO, cnt[dO])
            op("scalar", lambda E, i=ibuf, o=obuf: E.sign(o, i), inc="S")
            sign_S_hist.append(cnt["S"])
            wait("scalar", "S", cnt["S"])
            op("scalar", lambda E: E.drain())
            k = r0 // 128
            m0 = c0 // 128
            out_ap = bass.AP(dst.tensor, (m0 * 128) * dcols + k * 128,
                             [[dcols, 128], [128 * dcols, w // 128], [1, 128]])
            dma_k("scalar", dO, out_ap, obuf)
            yield

    sign_S_hist = []

    def layernorm_chunk(eps, ring2=True):
        wait("tensor", "V", cnt["V"])
        wait("scalar", "dC", cnt["dC"])
        sq_T = []
        nring = 2 if ring2 else 4
        for k in range(KT):
            xk = big1[:, k * CH:(k + 1) * CH]
            sq = (sqv(k % 2) if ring2
                  else [sqv(0), sqv(1), delta(0), delta(1)][k % 4])
            if k >= nring:
                wait("scalar", "T", sq_T[k - nring])
            op("scalar", lambda E, a=sq, b=xk: E.activation(a, b, AF.Square), inc="S")
            op("tensor", lambda E, b=xk, k=k: E.matmul(
                prow[0:1, 0:CH], onc[:], b, start=(k == 0), stop=(k == KT - 1)))
            wait("tensor", "S", cnt["S"])
            op("tensor", lambda E, a=sq, k=k: E.matmul(
                prow[0:1, CH:2 * CH], onc[:], a, start=(k == 0), stop=(k == KT - 1)),
                inc="T")
            sq_T.append(cnt["T"])
        wait("vector", "T", cnt["T"])
        op("vector", lambda E: E.tensor_scalar(row(0), prow[0:1, 0:CH], 1.0 / D, None, ALU.mult))
        op("vector", lambda E: E.tensor_scalar(row(1), prow[0:1, CH:2 * CH], 1.0 / D, None, ALU.mult))
        op("vector", lambda E: E.tensor_tensor(row(2), row(0), row(0), ALU.mult))
        op("vector", lambda E: E.tensor_tensor(row(1), row(1), row(2), ALU.subtract))
        op("vector", lambda E: E.tensor_scalar(row(1), row(1), eps, None, ALU.add))
        op("vector", lambda E: E.reciprocal(row(2), row(1)), inc="V")
        wait("scalar", "V", cnt["V"])
        op("scalar", lambda E: E.activation(row(1), row(2), AF.Sqrt), inc="S")
        wait("vector", "S", cnt["S"])
        op("vector", lambda E: E.tensor_tensor(row(2), row(0), row(1), ALU.mult), inc="V")
        wait("tensor", "V", cnt["V"])
        op("tensor", lambda E: E.matmul(pb[4][:], onr[:], row(1), start=True, stop=True))
        op("tensor", lambda E: E.matmul(pb[5][:], onr[:], row(2), start=True, stop=True), inc="T")
        wait("vector", "T", cnt["T"])
        for k in range(KT):
            xk = big1[:, k * CH:(k + 1) * CH]
            op("vector", lambda E, a=xk: E.tensor_tensor(a, a, pb[4][:], ALU.mult))
            op("vector", lambda E, a=xk: E.tensor_tensor(a, a, pb[5][:], ALU.subtract))
        op("vector", lambda E: E.tensor_copy(row(0), row(0)), inc="V")

    def ssm_chunk(par):
        btv, ctv, dbv = ssm_bufs(par)
        wait("tensor", "V", cnt["V"])
        wait("vector", "dR0", cnt["dR0"])
        wait("vector", "dR1", cnt["dR1"])
        specs = ((0, 128, pb[0], btv), (128, 256, pb[1], ctv), (256, 288, pb[0], dbv))
        for gi, (m0, m1, pdst, sdst) in enumerate(specs):
            mw = m1 - m0
            if gi == 2:
                wait("tensor", "V", cnt["V"])
            for k in range(KT):
                stat = wx_r[:, k * 288 + m0:k * 288 + m1]
                mov = big1[:, k * CH:(k + 1) * CH]
                op("tensor", lambda E, s=stat, m=mov, k=k, pd=pdst, mw=mw: E.matmul(
                    pd[0:mw, :], s, m, start=(k == 0), stop=(k == KT - 1)),
                    inc=("T" if k == KT - 1 else None))
            wait("vector", "T", cnt["T"])
            op("vector", lambda E, pd=pdst, sd=sdst, mw=mw: E.tensor_copy(sd, pd[0:mw, :]),
               inc="V")

    def scan_gen(want_y, yoff, sbpar, dbrows=(2, 3), ystrows=(1, 4)):
        btv, ctv, dbv = ssm_bufs(sbpar)
        zadd_v, dbmm_T, ymm_T = [], [], []
        for h in range(NH):
            par = h % 2
            dl, bl, hl = delta(par), bsq(par), hst(par)
            pdb = pb[2 + par]
            pyy = prow[0:1, par * CH:(par + 1) * CH]
            dbr = row(dbrows[par])
            if h >= 2:
                wait("gpsimd", "T", dbmm_T[h - 2])
            else:
                wait("gpsimd", "V", cnt["V"])
            dR = "dR0" if par == 0 else "dR1"
            op("gpsimd", lambda E, d=dbr, h=h: E.dma_start(d, dbv[h:h + 1, :]), inc=dR)
            if h >= 2:
                wait("tensor", "V", zadd_v[h - 2])
            else:
                wait("tensor", "V", cnt["V"])
            wait("tensor", dR, cnt[dR])
            op("tensor", lambda E, p=pdb, d=dbr: E.matmul(
                p[:], onr[:], d, start=True, stop=True), inc="T")
            dbmm_T.append(cnt["T"])
            wait("vector", "T", cnt["T"])
            op("vector", lambda E, d=dl, p=pdb, b=btv: E.tensor_tensor(d, p[:], b, ALU.add),
               inc="V")
            zadd_v.append(cnt["V"])
            wait("scalar", "V", cnt["V"])
            op("scalar", lambda E, d=dl: E.activation(d, d, AF.Exp))
            op("scalar", lambda E, d=dl: E.activation(d, d, AF.Ln, bias=1.0), inc="S")
            wait("vector", "S", cnt["S"])
            op("vector", lambda E, b=bl, d=dl: E.tensor_tensor(b, d, d, ALU.mult), inc="V")
            wait("scalar", "V", cnt["V"])
            op("scalar", lambda E, d=dl, h=h: E.activation(
                d, d, AF.Exp, scale=a_r[:, h:h + 1]), inc="S")
            wait("vector", "S", cnt["S"])
            op("vector", lambda E, hh=hl, a=dl, b=bl, h=h: E.tensor_tensor_scan(
                hh, a, b, st[:, h:h + 1], ALU.mult, ALU.add))
            op("vector", lambda E: E.drain())
            op("vector", lambda E, hh=hl, h=h: E.tensor_copy(st[:, h:h + 1], hh[:, CH - 1:CH]),
               inc="V")
            if want_y:
                ystg = row(ystrows[par])
                op("vector", lambda E, t=bl, hh=hl, c=ctv: E.tensor_tensor(t, hh, c, ALU.mult),
                   inc="V")
                wait("tensor", "V", cnt["V"])
                op("tensor", lambda E, p=pyy, t=bl: E.matmul(
                    p, onc[:], t, start=True, stop=True), inc="T")
                dY = "dY0" if par == 0 else "dY1"
                wait("vector", "T", cnt["T"])
                if h >= 2:
                    wait("vector", dY, ymm_T[h - 2][1])
                op("vector", lambda E, p=pyy, s=ystg: E.tensor_copy(s, p), inc="V")
                wait("gpsimd", "V", cnt["V"])
                op("gpsimd", lambda E, s=ystg, h=h, yoff=yoff: E.dma_start(
                    y_own[h:h + 1, yoff:yoff + CH], s), inc=dY)
                ymm_T.append((dY, cnt[dY]))
            yield
        op("vector", lambda E: E.tensor_copy(row(0), row(0)), inc="V")

    def gate_gen(j):
        gate_T = []
        for m in range(KT):
            par = m % 2
            wo_off = par * KT * 128
            pg = pb[par]
            if m >= 2:
                wait("sync", "T", gate_T[m - 2])
            else:
                wait("sync", "T", cnt["T"])
            dF = "dF0" if par == 0 else "dF1"
            dma_k("sync", dF, wfp[:, wo_off:wo_off + KT * 128],
                  hd["wg"][m * 128:(m + 1) * 128, :])
            wait("tensor", dF, cnt[dF])
            if m >= 2:
                wait("tensor", "S", sigm_S[m - 2])
            else:
                wait("tensor", "V", cnt["V"])
            for k in range(KT):
                op("tensor", lambda E, o=wo_off, k=k, p=pg: E.matmul(
                    p[:], wfp[:, o + k * 128:o + (k + 1) * 128],
                    big1[:, k * CH:(k + 1) * CH],
                    start=(k == 0), stop=(k == KT - 1)), inc=("T" if k == KT - 1 else None))
            gate_T.append(cnt["T"])
            wait("scalar", "T", cnt["T"])
            op("scalar", lambda E, p=pg, m=m: E.activation(
                big2[:, m * CH:(m + 1) * CH], p[:], AF.Sigmoid, bias=bg_r[:, m:m + 1]),
                inc="S")
            sigm_S.append(cnt["S"])
            yield

    sigm_S = []

    def quant_gen(j, round_V):
        wait("tensor", "V", cnt["V"])
        wait("scalar", "V", cnt["V"])
        wait("scalar", "dY0", cnt["dY0"])
        wait("scalar", "dY1", cnt["dY1"])
        qsq_T = []
        for k in range(KT):
            xk = big1[:, k * CH:(k + 1) * CH]
            sq = [sqv(0), sqv(1), delta(0), delta(1)][k % 4]
            if k >= 4:
                wait("scalar", "T", qsq_T[k - 4])
            op("scalar", lambda E, a=sq, b=xk: E.activation(a, b, AF.Square), inc="S")
            op("tensor", lambda E, b=xk, k=k: E.matmul(
                prow[0:1, 0:CH], onc[:], b, start=(k == 0), stop=(k == KT - 1)))
            wait("tensor", "S", cnt["S"])
            op("tensor", lambda E, a=sq, k=k: E.matmul(
                prow[0:1, CH:2 * CH], onc[:], a, start=(k == 0), stop=(k == KT - 1)),
                inc="T")
            qsq_T.append(cnt["T"])
            if k % 4 == 3:
                yield
        wait("vector", "T", cnt["T"])
        op("vector", lambda E: E.tensor_scalar(row(0), prow[0:1, 0:CH], 1.0 / D, None, ALU.mult))
        op("vector", lambda E: E.tensor_scalar(row(1), prow[0:1, CH:2 * CH], 1.0 / D, None, ALU.mult))
        op("vector", lambda E: E.tensor_tensor(row(2), row(0), row(0), ALU.mult))
        op("vector", lambda E: E.tensor_tensor(row(1), row(1), row(2), ALU.subtract))
        op("vector", lambda E: E.tensor_scalar(row(1), row(1), EPS2, None, ALU.add))
        op("vector", lambda E: E.reciprocal(row(2), row(1)), inc="V")
        wait("scalar", "V", cnt["V"])
        op("scalar", lambda E: E.activation(row(1), row(2), AF.Sqrt), inc="S")
        wait("tensor", "S", cnt["S"])
        op("tensor", lambda E: E.matmul(pb[4][:], onr[:], row(0), start=True, stop=True),
           inc="T")
        wait("vector", "T", cnt["T"])
        yield
        for k in range(KT):
            xk = big1[:, k * CH:(k + 1) * CH]
            sq = sqv(0)
            op("vector", lambda E, a=sq, b=xk: E.tensor_tensor(a, b, pb[4][:], ALU.subtract))
            if k == 0:
                op("vector", lambda E, a=sq: E.tensor_copy(mtl[:, :], a))
                op("vector", lambda E, a=sq: E.tensor_copy(mxc[:, :], a))
            else:
                op("vector", lambda E, a=sq: E.tensor_tensor(mtl[:, :], mtl[:, :], a, ALU.max))
                op("vector", lambda E, a=sq: E.tensor_tensor(mxc[:, :], mxc[:, :], a, ALU.min))
            if k % 4 == 3:
                yield
        op("vector", lambda E: E.tensor_scalar(mxc[:, :], mxc[:, :], -1.0, None, ALU.mult))
        op("vector", lambda E: E.tensor_tensor(mtl[:, :], mtl[:, :], mxc[:, :], ALU.max))
        op("vector", lambda E: E.tensor_copy(row(0), row(0)), inc="V")
        wait("tensor", "V", cnt["V"])
        wait("tensor", "G", g0_setup)
        for q in range(4):
            op("tensor", lambda E, q=q: E.transpose(
                pb[5][:, q * 128:(q + 1) * 128], mtl[:, q * 128:(q + 1) * 128], ident[:]),
                inc=("T" if q == 3 else None))
        wait("vector", "T", cnt["T"])
        for q in range(4):
            op("vector", lambda E, q=q: E.reduce_max(
                sqv(0)[:, q:q + 1], pb[5][:, q * 128:(q + 1) * 128], mybir.AxisListType.X))
        op("vector", lambda E: E.tensor_copy(row(0)[0:1, 0:1], row(0)[0:1, 0:1]), inc="V")
        wait("gpsimd", "V", cnt["V"])
        sq0 = sqv(0)
        r3 = row(3)
        for q in range(4):
            op("gpsimd", lambda E, q=q: E.dma_start(
                bass.AP(r3.tensor, r3.offset + q * 128, [[r3.ap[0][0], 1], [1, 128]]),
                bass.AP(sq0.tensor, sq0.offset + q, [[sq0.ap[0][0], 128], [1, 1]]),
                allow_slow_non_contiguous=True), inc="dQ")
        wait("vector", "dQ", cnt["dQ"])
        op("vector", lambda E: E.reciprocal(row(2), row(3)))
        op("vector", lambda E: E.tensor_scalar(row(4), row(2), 127.0, None, ALU.mult))
        op("vector", lambda E: E.tensor_tensor(row(5), row(0), row(4), ALU.mult))
        op("vector", lambda E: E.tensor_scalar(row(5), row(5), -1.0, None, ALU.mult))
        op("vector", lambda E: E.tensor_scalar(row(6), row(3), 1.0 / 127.0, None, ALU.mult))
        op("vector", lambda E: E.tensor_tensor(row(6), row(6), row(1), ALU.mult), inc="V")
        wait("tensor", "V", cnt["V"])
        op("tensor", lambda E: E.matmul(pb[4][:], onr[:], row(6), start=True, stop=True), inc="T")
        wait("vector", "T", cnt["T"])
        op("vector", lambda E: E.tensor_copy(mxc[:, :], pb[4][:]), inc="V")
        wait("tensor", "V", cnt["V"])
        op("tensor", lambda E: E.matmul(pb[4][:], onr[:], row(4), start=True, stop=True))
        op("tensor", lambda E: E.matmul(pb[5][:], onr[:], row(5), start=True, stop=True), inc="T")
        wait("vector", "T", cnt["T"])
        yield
        for k in range(KT):
            xk = big1[:, k * CH:(k + 1) * CH]
            vv, dd = sqv(0), sqv(1)
            xqs = xqb[:, k * CH:(k + 1) * CH]
            op("vector", lambda E, v=vv, x=xk: E.tensor_tensor(v, x, pb[4][:], ALU.mult))
            op("vector", lambda E, v=vv: E.tensor_tensor(v, v, pb[5][:], ALU.add))
            op("vector", lambda E, v=vv: E.tensor_copy(i32t[:, :], v))
            op("vector", lambda E, d=dd: E.tensor_copy(d, i32t[:, :]))
            op("vector", lambda E, v=vv, d=dd: E.tensor_tensor(v, v, d, ALU.subtract))
            op("vector", lambda E, v=vv: E.tensor_scalar(mtl[:, :], v, 0.5, None, ALU.is_gt))
            op("vector", lambda E, v=vv: E.tensor_scalar(v, v, -0.5, None, ALU.is_lt))
            op("vector", lambda E, v=vv: E.tensor_tensor(mtl[:, :], mtl[:, :], v, ALU.subtract))
            op("vector", lambda E, d=dd: E.tensor_tensor(d, d, mtl[:, :], ALU.add))
            op("vector", lambda E, x=xqs, d=dd: E.tensor_scalar(
                x, d, -128.0, 127.0, ALU.max, ALU.min))
            round_V.append(cnt["V"])
            if k % 2 == 1:
                yield
        op("vector", lambda E: E.tensor_copy(row(0)[0:1, 0:1], row(0)[0:1, 0:1]), inc="V")

    def yfwo_gen(j, round_V):
        ysl = y_own[:, j * CH:(j + 1) * CH]
        yf_V = []
        for m in range(KT):
            par = m % 2
            pyb = pb[2 + par]
            if m == 0:
                wait("tensor", "dY0", cnt["dY0"])
                wait("tensor", "dY1", cnt["dY1"])
                wait("tensor", "G", g0_setup)
                wait("tensor", "S", cnt["S"])
            if m >= 2:
                wait("tensor", "V", yf_V[m - 2])
            op("tensor", lambda E, m=m, p=pyb: E.matmul(
                p[:], ehb[:, m * 128:(m + 1) * 128], ysl, start=True, stop=True), inc="T")
            wait("vector", "T", cnt["T"])
            xk = big1[:, m * CH:(m + 1) * CH]
            gk = big2[:, m * CH:(m + 1) * CH]
            tv = bsq(0)
            op("vector", lambda E, x=xk, p=pyb, m=m, t=tv: E.scalar_tensor_tensor(
                t, x, dp_r[:, m:m + 1], p[:], ALU.mult, ALU.add))
            op("vector", lambda E, g=gk, t=tv: E.tensor_tensor(g, g, t, ALU.mult), inc="V")
            yf_V.append(cnt["V"])
            if m % 4 == 3:
                yield
        for _ in wo_gen(j, round_V):
            yield

    def wo_gen(j, round_V):
        wo_T = []
        acc_V = []
        for m in range(KT):
            bank = m % 4
            wo_off = (m % 2) * KT * 128
            pg = pb[bank]
            if m >= 2:
                wait("sync", "T", wo_T[m - 2])
            else:
                wait("sync", "V", cnt["V"])
                wait("sync", "T", cnt["T"])
            dF = "dF0" if m % 2 == 0 else "dF1"
            dma_k("sync", dF, wfp[:, wo_off:wo_off + KT * 128],
                  hd["wo"][m * 128:(m + 1) * 128, :])
            wait("tensor", dF, cnt[dF])
            if m >= 4:
                wait("tensor", "V", acc_V[m - 4])
            else:
                wait("tensor", "V", cnt["V"])
            for k in range(KT):
                op("tensor", lambda E, o=wo_off, k=k, p=pg: E.matmul(
                    p[:], wfp[:, o + k * 128:o + (k + 1) * 128],
                    big2[:, k * CH:(k + 1) * CH],
                    start=(k == 0), stop=(k == KT - 1)), inc=("T" if k == KT - 1 else None))
            wo_T.append(cnt["T"])
            while len(round_V) <= m:
                yield
            wait("vector", "T", wo_T[m])
            ok = big1[:, m * CH:(m + 1) * CH]
            op("vector", lambda E, o=ok, p=pg: E.tensor_copy(o, p[:]), inc="V")
            acc_V.append(cnt["V"])
            yield

    def load_x_chunk(src, j):
        wait("sync", "V", cnt["V"])
        wait("sync", "T", cnt["T"])
        wait("sync", "dC", cnt["dC"])
        for k in range(KT):
            dma_k("sync", "dX", big1[:, k * CH:(k + 1) * CH],
                  src[k * 128:(k + 1) * 128, j * CH:(j + 1) * CH])
        wait("scalar", "dX", cnt["dX"])
        wait("tensor", "dX", cnt["dX"])
        wait("vector", "dX", cnt["dX"])

    sgen = sign_gen()

    def prep_gen(j):
        load_x_chunk(hd["xpre"], j)
        yield
        layernorm_chunk(EPS1)
        yield
        ssm_chunk(j % 2)
        yield

    if PIPE_PREFIX:
        drive(prep_gen(0), bg=sgen, bg_n=2)
        for j in range(NCH):
            gens = [scan_gen(False, 0, j % 2, dbrows=(3, 4))]
            if j + 1 < NCH:
                gens.append(prep_gen(j + 1))
            drive(*gens, bg=sgen, bg_n=2)
    else:
        for j in range(NCH):
            drive(prep_gen(j))
            drive(scan_gen(False, 0, j % 2, dbrows=(3, 4)))
    op("vector", lambda E: E.tensor_scalar(st[:, :], st[:, :], msk[:, 0:1], None, ALU.mult),
       inc="V")
    drive(sgen)
    sign_done = (cnt["dO0"], cnt["dO1"])

    for j in range(NCH):
        load_x_chunk(hd["xown"], j)
        layernorm_chunk(EPS1, ring2=False)
        ssm_chunk(0)
        sigm_S.clear()
        if PIPE_SCANGATE:
            drive(scan_gen(True, j * CH, 0), gate_gen(j))
        else:
            drive(scan_gen(True, j * CH, 0))
            drive(gate_gen(j))
        round_V = []
        if PIPE_WOQUANT:
            drive(yfwo_gen(j, round_V), quant_gen(j, round_V))
        else:
            drive(yfwo_gen(j, round_V))
            drive(quant_gen(j, round_V))
        wait("sync", "dO0", sign_done[0])
        wait("sync", "dO1", sign_done[1])
        wait("gpsimd", "dO0", sign_done[0])
        wait("gpsimd", "dO1", sign_done[1])
        wait("vector", "dO0", sign_done[0])
        wait("vector", "dO1", sign_done[1])
        wait("tensor", "V", cnt["V"])
        gu_T = []
        ev_hist = []
        psplit_hist = []
        wd_last_T = []
        accD_hist = []
        xres_k = [0]
        xadd_V = []

        def xres_step():
            k = xres_k[0]
            if k >= KT:
                return
            xres_k[0] += 1
            stg = mtl[:, :] if k % 2 == 0 else gt2[:, :]
            if k >= 2:
                wait("scalar", "V", xadd_V[k - 2])
            else:
                wait("scalar", "V", cnt["V"])
            dCp = "dC0" if k % 2 == 0 else "dC1"
            op("scalar", lambda E, k=k, s=stg: E.dma_start(
                s, hd["xown"][k * 128:(k + 1) * 128, j * CH:(j + 1) * CH]), inc=dCp)
            wait("vector", dCp, cnt[dCp])
            ok = big1[:, k * CH:(k + 1) * CH]
            op("vector", lambda E, o=ok, s=stg: E.tensor_tensor(o, o, s, ALU.add), inc="V")
            xadd_V.append(cnt["V"])

        def gubuf(g):
            return scrA if g % 2 == 0 else big2

        def gu_part(g):
            buf = gubuf(g)
            for mi in range(GMT):
                hm = g * GMT + mi
                par2 = hm % 2
                og = (2 * par2) * KT * 128
                ou = (2 * par2 + 1) * KT * 128
                if len(gu_T) >= 2:
                    wait("sync", "T", gu_T[-2])
                dU = "dU0" if par2 == 0 else "dU1"
                dma_k("sync", dU, wgu[:, og:og + KT * 128],
                      hd["sg_s"][hm * 128:(hm + 1) * 128, :])
                dma_k("sync", dU, wgu[:, ou:ou + KT * 128],
                      hd["su_s"][hm * 128:(hm + 1) * 128, :])
                pgg, puu = pb[mi % 2], pb[2 + mi % 2]
                wait("tensor", dU, cnt[dU])
                if mi >= 2:
                    wait("tensor", "V", ev_hist[-2])
                elif g > 0 and ev_hist:
                    wait("tensor", "V", ev_hist[-1])
                for k in range(KT):
                    op("tensor", lambda E, o=og, k=k, p=pgg: E.matmul(
                        p[:], wgu[:, o + k * 128:o + (k + 1) * 128],
                        xqb[:, k * CH:(k + 1) * CH],
                        start=(k == 0), stop=(k == KT - 1)))
                for k in range(KT):
                    op("tensor", lambda E, o=ou, k=k, p=puu: E.matmul(
                        p[:], wgu[:, o + k * 128:o + (k + 1) * 128],
                        xqb[:, k * CH:(k + 1) * CH],
                        start=(k == 0), stop=(k == KT - 1)),
                        inc=("T" if k == KT - 1 else None))
                gu_T.append(cnt["T"])
                wait("vector", "T", cnt["T"])
                gsl = buf[:, mi * CH:(mi + 1) * CH]
                usl = buf[:, (GMT + mi) * CH:(GMT + mi + 1) * CH]
                op("vector", lambda E, a=gsl, p=pgg: E.tensor_tensor(a, p[:], mxc[:, :], ALU.mult))
                op("vector", lambda E, a=usl, p=puu: E.tensor_copy(a, p[:]), inc="V")
                ev_hist.append(cnt["V"])
                if mi % 2 == 1:
                    xres_step()

        def pchain(g):
            buf = gubuf(g)
            gall = buf[:, 0:GMT * CH]
            uall = buf[:, GMT * CH:2 * GMT * CH]
            wait("scalar", "V", ev_hist[-1])
            op("scalar", lambda E, a=gall: E.activation(a, a, AF.Silu), inc="S")
            wait("vector", "S", cnt["S"])
            if len(wd_last_T) >= 1:
                wait("vector", "T", wd_last_T[-1])
            op("vector", lambda E, a=gall, b=uall: E.tensor_tensor(a, a, b, ALU.mult))
            op("vector", lambda E, a=gall: E.tensor_tensor(
                a.rearrange("p (t c) -> p t c", c=CH),
                a.rearrange("p (t c) -> p t c", c=CH),
                mxc[:, :].unsqueeze(1).broadcast_to([128, GMT, CH]), ALU.mult))
            op("vector", lambda E, a=gall: E.tensor_copy(p_hi[:, :], a))
            op("vector", lambda E, a=gall: E.tensor_tensor(p_lo[:, :], a, p_hi[:, :], ALU.subtract),
               inc="V")
            psplit_hist.append(cnt["V"])

        def wd_part(g):
            wd_T = []
            for m in range(KT):
                par3 = m % 2
                od = par3 * GMT * 128
                if m >= 2:
                    wait("gpsimd", "T", wd_T[m - 2])
                elif wd_last_T:
                    wait("gpsimd", "T", wd_last_T[-1])
                dma_k("gpsimd", "dW0" if par3 == 0 else "dW1",
                      wdd[:, od:od + GMT * 128],
                      hd["sd_s"][m * 128:(m + 1) * 128, g * KG:(g + 1) * KG])
                pdd = pb[4 + par3]
                wait("tensor", "dW0" if par3 == 0 else "dW1",
                     cnt["dW0" if par3 == 0 else "dW1"])
                if m == 0:
                    wait("tensor", "V", psplit_hist[-1])
                    if len(accD_hist) >= 1:
                        wait("tensor", "V", accD_hist[-1])
                elif m == 1 and len(accD_hist) >= 1:
                    wait("tensor", "V", accD_hist[-1])
                elif m >= 2:
                    wait("tensor", "V", accD_hist[-(KT - m + 2)] if False else accD_now[m - 2])
                for kk in range(GMT):
                    op("tensor", lambda E, o=od, kk=kk, p=pdd: E.matmul(
                        p[:], wdd[:, o + kk * 128:o + (kk + 1) * 128],
                        p_hi[:, kk * CH:(kk + 1) * CH],
                        start=(kk == 0), stop=False))
                for kk in range(GMT):
                    op("tensor", lambda E, o=od, kk=kk, p=pdd: E.matmul(
                        p[:], wdd[:, o + kk * 128:o + (kk + 1) * 128],
                        p_lo[:, kk * CH:(kk + 1) * CH],
                        start=False, stop=(kk == GMT - 1)),
                        inc=("T" if kk == GMT - 1 else None))
                wd_T.append(cnt["T"])
                wait("vector", "T", cnt["T"])
                ok = big1[:, m * CH:(m + 1) * CH]
                op("vector", lambda E, o=ok, p=pdd: E.tensor_tensor(o, o, p[:], ALU.add), inc="V")
                accD_now.append(cnt["V"])
            wd_last_T.append(wd_T[-1])
            accD_hist.append(accD_now[-1])

        if PIPE_FFN:
            gu_part(0)
            for g in range(1, NGRP):
                pchain(g - 1)
                accD_now = []
                gu_part(g)
                wd_part(g - 1)
            pchain(NGRP - 1)
            accD_now = []
            wd_part(NGRP - 1)
        else:
            for g in range(NGRP):
                gu_part(g)
                pchain(g)
                accD_now = []
                wd_part(g)
        while xres_k[0] < KT:
            xres_step()
        wait("scalar", "V", cnt["V"])
        for k in range(KT):
            dma("scalar", hd["out_t"][k * 128:(k + 1) * 128, j * CH:(j + 1) * CH],
                big1[:, k * CH:(k + 1) * CH])

    wait("gpsimd", "dC", cnt["dC"])
    op("gpsimd", lambda E: E.memset(st[0:1, 0:1], 0.0), inc="G")


_NC_CACHE = {}


def kernel(L0_attn_x, L0_attn_Wx, L0_attn_Wg, L0_attn_bg, L0_attn_Wo,
           L0_attn_A, L0_attn_Dp, L0_ffn_W_g, L0_ffn_W_u, L0_ffn_W_d):
    x = np.ascontiguousarray(np.asarray(L0_attn_x, dtype=np.float32))
    wx = np.ascontiguousarray(np.asarray(L0_attn_Wx, dtype=np.float32))
    wg = np.ascontiguousarray(np.asarray(L0_attn_Wg, dtype=np.float32))
    bg = np.ascontiguousarray(np.asarray(L0_attn_bg, dtype=np.float32))
    wo = np.ascontiguousarray(np.asarray(L0_attn_Wo, dtype=np.float32))
    a = np.ascontiguousarray(np.asarray(L0_attn_A, dtype=np.float32))
    dp = np.ascontiguousarray(np.asarray(L0_attn_Dp, dtype=np.float32))
    w_g = np.ascontiguousarray(np.asarray(L0_ffn_W_g, dtype=np.float32))
    w_u = np.ascontiguousarray(np.asarray(L0_ffn_W_u, dtype=np.float32))
    w_d = np.ascontiguousarray(np.asarray(L0_ffn_W_d, dtype=np.float32))

    xT = np.ascontiguousarray(x.transpose(0, 2, 1))
    wg_st = np.ascontiguousarray(
        wg.reshape(KT, 128, KT, 128).transpose(2, 1, 0, 3).reshape(D, D))
    wo_st = np.ascontiguousarray(
        wo.reshape(KT, 128, KT, 128).transpose(2, 1, 0, 3).reshape(D, D))
    wgtT = np.ascontiguousarray(w_g.T)
    wutT = np.ascontiguousarray(w_u.T)
    wdtT = np.ascontiguousarray(w_d.T)
    a_tT = np.ascontiguousarray(a.T)
    dp_til = np.ascontiguousarray(np.repeat(dp, HD).reshape(KT, 128).T)
    bg_til = np.ascontiguousarray(bg.reshape(KT, 128).T)
    zeros_pre = np.zeros((D, 2048), dtype=np.float32)

    if "nc" not in _NC_CACHE:
        _NC_CACHE["nc"] = build_nc()
    nc = _NC_CACHE["nc"]

    in_maps = []
    for c in range(8):
        b, half = c // 2, c % 2
        xpre = zeros_pre if half == 0 else np.ascontiguousarray(xT[b, :, 0:2048])
        xown = np.ascontiguousarray(xT[b, :, half * 2048:(half + 1) * 2048])
        m = np.full((128, 1), 0.0 if half == 0 else 1.0, dtype=np.float32)
        in_maps.append({
            "xpre": xpre, "xown": xown, "smask": m,
            "wx": wx, "wg": wg_st, "wo": wo_st,
            "wgt": wgtT, "wut": wutT, "wdt": wdtT,
            "a_t": a_tT, "dp_t": dp_til, "bg_t": bg_til,
        })
    res = run_bass_kernel_spmd(nc, in_maps, core_ids=list(range(8)))
    out = np.empty((B, L, D), dtype=np.float32)
    for c in range(8):
        b, half = c // 2, c % 2
        out[b, half * 2048:(half + 1) * 2048, :] = res.results[c]["out_t"].T
    return out


# version 09 (final)
# speedup vs baseline: 1.0742x; 1.0563x over previous
import numpy as np
from contextlib import ExitStack

import concourse.bass as bass
import concourse.mybir as mybir
from concourse.bass_utils import run_bass_kernel_spmd

F32 = mybir.dt.float32
BF16 = mybir.dt.bfloat16
I32 = mybir.dt.int32
AF = mybir.ActivationFunctionType
ALU = mybir.AluOpType

B, L, D = 4, 4096, 2048
NH, NS, HD = 32, 128, 64
HF = 8192
KT = D // 128
CH = 512
NCH = 2048 // CH
KG = 1024
NGRP = HF // KG
GMT = KG // 128
EPS1 = 1e-5
EPS2 = 1e-8

import os
PIPE_PREFIX = os.environ.get("KPIPE_PREFIX", "1") == "1"
PIPE_SCANGATE = os.environ.get("KPIPE_SCANGATE", "1") == "1"
PIPE_WOQUANT = os.environ.get("KPIPE_WOQUANT", "1") == "1"
PIPE_FFN = os.environ.get("KPIPE_FFN", "1") == "1"

ENGINES = ("sync", "gpsimd", "scalar", "vector", "tensor")
SEM_ENG = {"dS": "sync", "dG": "gpsimd", "dC": "scalar",
           "dR0": "gpsimd", "dR1": "gpsimd", "dY0": "gpsimd", "dY1": "gpsimd",
           "dQ": "gpsimd", "dW0": "gpsimd", "dW1": "gpsimd",
           "dI0": "sync", "dI1": "sync", "dX": "sync",
           "dF0": "sync", "dF1": "sync", "dU0": "sync", "dU1": "sync",
           "dO0": "scalar", "dO1": "scalar", "dC0": "scalar", "dC1": "scalar",
           "V": "vector", "S": "scalar", "T": "tensor", "G": "gpsimd"}


def build_nc():
    nc = bass.Bass("TRN2", target_bir_lowering=False, detect_race_conditions=False)

    hd = {}
    hd["xpre"] = nc.dram_tensor("xpre", [D, 2048], F32, kind="ExternalInput").ap()
    hd["xown"] = nc.dram_tensor("xown", [D, 2048], F32, kind="ExternalInput").ap()
    hd["smask"] = nc.dram_tensor("smask", [128, 1], F32, kind="ExternalInput").ap()
    hd["wx"] = nc.dram_tensor("wx", [D, 288], F32, kind="ExternalInput").ap()
    hd["wg"] = nc.dram_tensor("wg", [D, D], F32, kind="ExternalInput").ap()
    hd["wo"] = nc.dram_tensor("wo", [D, D], F32, kind="ExternalInput").ap()
    hd["wgt"] = nc.dram_tensor("wgt", [D, HF], F32, kind="ExternalInput").ap()
    hd["wut"] = nc.dram_tensor("wut", [D, HF], F32, kind="ExternalInput").ap()
    hd["wdt"] = nc.dram_tensor("wdt", [HF, D], F32, kind="ExternalInput").ap()
    hd["a_t"] = nc.dram_tensor("a_t", [NS, NH], F32, kind="ExternalInput").ap()
    hd["dp_t"] = nc.dram_tensor("dp_t", [128, KT], F32, kind="ExternalInput").ap()
    hd["bg_t"] = nc.dram_tensor("bg_t", [128, KT], F32, kind="ExternalInput").ap()
    hd["out_t"] = nc.dram_tensor("out_t", [D, 2048], F32, kind="ExternalOutput").ap()
    hd["sg_s"] = nc.dram_tensor("sg_s", [HF, D], BF16, kind="Internal").ap()
    hd["su_s"] = nc.dram_tensor("su_s", [HF, D], BF16, kind="Internal").ap()
    hd["sd_s"] = nc.dram_tensor("sd_s", [D, HF], BF16, kind="Internal").ap()

    with ExitStack() as ctx:
        e = ctx.enter_context
        hd["big1"] = e(nc.sbuf_tensor("big1", [128, KT * CH], F32))
        hd["big2"] = e(nc.sbuf_tensor("big2", [128, KT * CH], F32))
        hd["xq"] = e(nc.sbuf_tensor("xq", [128, KT * CH], BF16))
        hd["scrA"] = e(nc.sbuf_tensor("scrA", [128, 2 * GMT * CH], F32))
        hd["p_hi"] = e(nc.sbuf_tensor("p_hi", [128, GMT * CH], BF16))
        hd["p_lo"] = e(nc.sbuf_tensor("p_lo", [128, GMT * CH], BF16))
        hd["wx_r"] = e(nc.sbuf_tensor("wx_r", [128, KT * 288], F32))
        hd["wfp"] = e(nc.sbuf_tensor("wfp", [128, 2 * KT * 128], F32))
        hd["wgu"] = e(nc.sbuf_tensor("wgu", [128, 4 * KT * 128], BF16))
        hd["wdd"] = e(nc.sbuf_tensor("wdd", [128, 2 * GMT * 128], BF16))
        hd["mtl"] = e(nc.sbuf_tensor("mtl", [128, CH], F32))
        hd["mxc"] = e(nc.sbuf_tensor("mxc", [128, CH], F32))
        hd["gt2"] = e(nc.sbuf_tensor("gt2", [128, CH], F32))
        hd["i32t"] = e(nc.sbuf_tensor("i32t", [128, CH], I32))
        hd["y_own"] = e(nc.sbuf_tensor("y_own", [32, 2048], F32))
        hd["eh"] = e(nc.sbuf_tensor("eh", [32, D], F32))
        hd["ident"] = e(nc.sbuf_tensor("ident", [128, 128], F32))
        hd["a_r"] = e(nc.sbuf_tensor("a_r", [NS, NH], F32))
        hd["dp_r"] = e(nc.sbuf_tensor("dp_r", [128, KT], F32))
        hd["bg_r"] = e(nc.sbuf_tensor("bg_r", [128, KT], F32))
        hd["msk"] = e(nc.sbuf_tensor("msk", [128, 1], F32))
        hd["st"] = e(nc.sbuf_tensor("st", [128, NH], F32))
        hd["onc"] = e(nc.sbuf_tensor("onc", [128, 1], F32))
        hd["onr"] = e(nc.sbuf_tensor("onr", [1, 128], F32))
        hd["prow"] = e(nc.psum_tensor("prow", [1, 2 * CH], F32))
        hd["pb"] = [e(nc.psum_tensor(f"pb{i}", [128, CH], F32)) for i in range(6)]
        sems = {k: e(nc.semaphore(k)) for k in SEM_ENG}
        block = e(nc.Block())

        for role in ENGINES:
            def body(eng, role=role):
                walk(nc, eng, role, hd, sems)
            getattr(block, role)(body)
    return nc


def drive(*gens, bg=None, bg_n=1):
    live = list(gens)
    while live:
        for g in list(live):
            try:
                next(g)
            except StopIteration:
                live.remove(g)
        for _ in range(bg_n if bg is not None else 0):
            try:
                next(bg)
            except StopIteration:
                bg = None
                break


def walk(nc, eng, role, hd, sems):
    cnt = dict.fromkeys(SEM_ENG, 0)

    def op(engine, fn, inc=None):
        inst = fn(eng) if engine == role else None
        if inc is not None:
            step = 16 if inc.startswith("d") else 1
            cnt[inc] += step
            if engine == role:
                inst.then_inc(sems[inc], step)
        return inst

    def wait(engine, key, val=None):
        if engine == role:
            eng.wait_ge(sems[key], cnt[key] if val is None else val)

    def dma(queue, dst, src):
        qkey = {"sync": "dS", "gpsimd": "dG", "scalar": "dC"}[queue]
        op(queue, lambda E: E.dma_start(dst, src), inc=qkey)

    def dma_k(queue, key, dst, src):
        op(queue, lambda E: E.dma_start(dst, src), inc=key)

    big1, big2, xqb = hd["big1"], hd["big2"], hd["xq"]
    scrA, p_hi, p_lo = hd["scrA"], hd["p_hi"], hd["p_lo"]
    wx_r, wfp, wgu, wdd = hd["wx_r"], hd["wfp"], hd["wgu"], hd["wdd"]
    mtl, mxc, gt2, i32t = hd["mtl"], hd["mxc"], hd["gt2"], hd["i32t"]
    y_own, ehb, ident = hd["y_own"], hd["eh"], hd["ident"]
    a_r, dp_r, bg_r, msk, st = hd["a_r"], hd["dp_r"], hd["bg_r"], hd["msk"], hd["st"]
    onc, onr, prow, pb = hd["onc"], hd["onr"], hd["prow"], hd["pb"]

    delta = lambda p: scrA[:, p * CH:(p + 1) * CH]
    bsq = lambda p: scrA[:, (2 + p) * CH:(3 + p) * CH]
    hst = lambda p: scrA[:, (4 + p) * CH:(5 + p) * CH]
    sqv = lambda p: scrA[:, (6 + p) * CH:(7 + p) * CH]

    def ssm_bufs(par):
        if par == 0:
            return (scrA[:, 8 * CH:9 * CH], scrA[:, 9 * CH:10 * CH],
                    scrA[0:32, 10 * CH:11 * CH])
        return (mtl[:, :], mxc[:, :], gt2[0:32, :])

    def row(i):
        slot = 11 + i if i < 5 else i - 5
        return scrA[0:1, slot * CH:(slot + 1) * CH]

    dma_k("sync", "dX", a_r[:], hd["a_t"])
    dma_k("sync", "dX", dp_r[:], hd["dp_t"])
    dma_k("sync", "dX", bg_r[:], hd["bg_t"])
    dma_k("sync", "dX", msk[:], hd["smask"])
    for k in range(KT):
        dma_k("sync", "dX", wx_r[:, k * 288:(k + 1) * 288],
              hd["wx"][k * 128:(k + 1) * 128, :])
    op("vector", lambda E: E.memset(onc[:], 1.0))
    op("vector", lambda E: E.memset(onr[:], 1.0))
    op("vector", lambda E: E.memset(st[:, :], 0.0), inc="V")
    op("gpsimd", lambda E: E.memset(ident[:], 0.0), inc="G")
    wait("gpsimd", "G", cnt["G"])
    op("gpsimd", lambda E: E.affine_select(
        out=ident[:], in_=ident[:], compare_op=ALU.not_equal, fill=1.0, base=0,
        pattern=[[-1, 128]], channel_multiplier=1), inc="G")
    op("gpsimd", lambda E: E.memset(ehb[:], 1.0), inc="G")
    wait("gpsimd", "G", cnt["G"])
    op("gpsimd", lambda E: E.affine_select(
        out=ehb[:], in_=ehb[:], compare_op=ALU.is_ge, fill=0.0, base=0,
        pattern=[[1, D]], channel_multiplier=-HD), inc="G")
    wait("gpsimd", "G", cnt["G"])
    op("gpsimd", lambda E: E.affine_select(
        out=ehb[:], in_=ehb[:], compare_op=ALU.is_ge, fill=0.0, base=HD - 1,
        pattern=[[-1, D]], channel_multiplier=HD), inc="G")
    g0_setup = cnt["G"]

    def sign_gen():
        jobs = []
        for (src, dst, rows, cols, dcols) in (
                (hd["wgt"], hd["sg_s"], D, HF, D),
                (hd["wut"], hd["su_s"], D, HF, D),
                (hd["wdt"], hd["sd_s"], HF, D, HF)):
            for r0 in range(0, rows, 128):
                for c0 in range(0, cols, 4096):
                    jobs.append((src, dst, r0, c0, dcols, min(4096, cols - c0)))
        for idx, (src, dst, r0, c0, dcols, w) in enumerate(jobs):
            par = idx % 2
            dI = "dI0" if par == 0 else "dI1"
            dO = "dO0" if par == 0 else "dO1"
            ibuf = big2[:, par * 4096:par * 4096 + w]
            obuf = (p_hi if par == 0 else p_lo)[:, 0:w]
            if idx >= 2:
                wait("sync", "S", sign_S_hist[idx - 2])
            dma_k("sync", dI, ibuf, src[r0:r0 + 128, c0:c0 + w])
            wait("scalar", dI, cnt[dI])
            if idx >= 2:
                wait("scalar", dO, cnt[dO])
            op("scalar", lambda E, i=ibuf, o=obuf: E.sign(o, i), inc="S")
            sign_S_hist.append(cnt["S"])
            wait("scalar", "S", cnt["S"])
            op("scalar", lambda E: E.drain())
            k = r0 // 128
            m0 = c0 // 128
            out_ap = bass.AP(dst.tensor, (m0 * 128) * dcols + k * 128,
                             [[dcols, 128], [128 * dcols, w // 128], [1, 128]])
            dma_k("scalar", dO, out_ap, obuf)
            yield

    sign_S_hist = []

    def layernorm_chunk(eps, ring2=True):
        wait("tensor", "V", cnt["V"])
        wait("scalar", "dC", cnt["dC"])
        sq_T = []
        nring = 2 if ring2 else 4
        for k in range(KT):
            xk = big1[:, k * CH:(k + 1) * CH]
            sq = (sqv(k % 2) if ring2
                  else [sqv(0), sqv(1), delta(0), delta(1)][k % 4])
            if k >= nring:
                wait("scalar", "T", sq_T[k - nring])
            op("scalar", lambda E, a=sq, b=xk: E.activation(a, b, AF.Square), inc="S")
            op("tensor", lambda E, b=xk, k=k: E.matmul(
                prow[0:1, 0:CH], onc[:], b, start=(k == 0), stop=(k == KT - 1)))
            wait("tensor", "S", cnt["S"])
            op("tensor", lambda E, a=sq, k=k: E.matmul(
                prow[0:1, CH:2 * CH], onc[:], a, start=(k == 0), stop=(k == KT - 1)),
                inc="T")
            sq_T.append(cnt["T"])
        wait("vector", "T", cnt["T"])
        op("vector", lambda E: E.tensor_scalar(row(0), prow[0:1, 0:CH], 1.0 / D, None, ALU.mult))
        op("vector", lambda E: E.tensor_scalar(row(1), prow[0:1, CH:2 * CH], 1.0 / D, None, ALU.mult))
        op("vector", lambda E: E.tensor_tensor(row(2), row(0), row(0), ALU.mult))
        op("vector", lambda E: E.tensor_tensor(row(1), row(1), row(2), ALU.subtract))
        op("vector", lambda E: E.tensor_scalar(row(1), row(1), eps, None, ALU.add))
        op("vector", lambda E: E.reciprocal(row(2), row(1)), inc="V")
        wait("scalar", "V", cnt["V"])
        op("scalar", lambda E: E.activation(row(1), row(2), AF.Sqrt), inc="S")
        wait("vector", "S", cnt["S"])
        op("vector", lambda E: E.tensor_tensor(row(2), row(0), row(1), ALU.mult), inc="V")
        wait("tensor", "V", cnt["V"])
        op("tensor", lambda E: E.matmul(pb[4][:], onr[:], row(1), start=True, stop=True))
        op("tensor", lambda E: E.matmul(pb[5][:], onr[:], row(2), start=True, stop=True), inc="T")
        wait("vector", "T", cnt["T"])
        for k in range(KT):
            xk = big1[:, k * CH:(k + 1) * CH]
            op("vector", lambda E, a=xk: E.tensor_tensor(a, a, pb[4][:], ALU.mult))
            op("vector", lambda E, a=xk: E.tensor_tensor(a, a, pb[5][:], ALU.subtract))
        op("vector", lambda E: E.tensor_copy(row(0), row(0)), inc="V")

    def ssm_chunk(par):
        btv, ctv, dbv = ssm_bufs(par)
        wait("tensor", "V", cnt["V"])
        wait("vector", "dR0", cnt["dR0"])
        wait("vector", "dR1", cnt["dR1"])
        specs = ((0, 128, pb[0], btv), (128, 256, pb[1], ctv), (256, 288, pb[0], dbv))
        for gi, (m0, m1, pdst, sdst) in enumerate(specs):
            mw = m1 - m0
            if gi == 2:
                wait("tensor", "V", cnt["V"])
            for k in range(KT):
                stat = wx_r[:, k * 288 + m0:k * 288 + m1]
                mov = big1[:, k * CH:(k + 1) * CH]
                op("tensor", lambda E, s=stat, m=mov, k=k, pd=pdst, mw=mw: E.matmul(
                    pd[0:mw, :], s, m, start=(k == 0), stop=(k == KT - 1)),
                    inc=("T" if k == KT - 1 else None))
            wait("vector", "T", cnt["T"])
            op("vector", lambda E, pd=pdst, sd=sdst, mw=mw: E.tensor_copy(sd, pd[0:mw, :]),
               inc="V")

    def scan_gen(want_y, yoff, sbpar, dbrows=(2, 3), ystrows=(1, 4)):
        btv, ctv, dbv = ssm_bufs(sbpar)
        zadd_v, dbmm_T, ymm_T = [], [], []
        for h in range(NH):
            par = h % 2
            dl, bl, hl = delta(par), bsq(par), hst(par)
            pdb = pb[2 + par]
            pyy = prow[0:1, par * CH:(par + 1) * CH]
            dbr = row(dbrows[par])
            if h >= 2:
                wait("gpsimd", "T", dbmm_T[h - 2])
            else:
                wait("gpsimd", "V", cnt["V"])
            dR = "dR0" if par == 0 else "dR1"
            op("gpsimd", lambda E, d=dbr, h=h: E.dma_start(d, dbv[h:h + 1, :]), inc=dR)
            if h >= 2:
                wait("tensor", "V", zadd_v[h - 2])
            else:
                wait("tensor", "V", cnt["V"])
            wait("tensor", dR, cnt[dR])
            op("tensor", lambda E, p=pdb, d=dbr: E.matmul(
                p[:], onr[:], d, start=True, stop=True), inc="T")
            dbmm_T.append(cnt["T"])
            wait("vector", "T", cnt["T"])
            op("vector", lambda E, d=dl, p=pdb, b=btv: E.tensor_tensor(d, p[:], b, ALU.add),
               inc="V")
            zadd_v.append(cnt["V"])
            wait("scalar", "V", cnt["V"])
            op("scalar", lambda E, d=dl: E.activation(d, d, AF.Exp))
            op("scalar", lambda E, d=dl: E.activation(d, d, AF.Ln, bias=1.0))
            op("scalar", lambda E, b=bl, d=dl: E.activation(b, d, AF.Square))
            op("scalar", lambda E, d=dl, h=h: E.activation(
                d, d, AF.Exp, scale=a_r[:, h:h + 1]), inc="S")
            wait("vector", "S", cnt["S"])
            op("vector", lambda E, hh=hl, a=dl, b=bl, h=h: E.tensor_tensor_scan(
                hh, a, b, st[:, h:h + 1], ALU.mult, ALU.add))
            op("vector", lambda E: E.drain())
            op("vector", lambda E, hh=hl, h=h: E.tensor_copy(st[:, h:h + 1], hh[:, CH - 1:CH]),
               inc="V")
            if want_y:
                ystg = row(ystrows[par])
                op("vector", lambda E, t=bl, hh=hl, c=ctv: E.tensor_tensor(t, hh, c, ALU.mult),
                   inc="V")
                wait("tensor", "V", cnt["V"])
                op("tensor", lambda E, p=pyy, t=bl: E.matmul(
                    p, onc[:], t, start=True, stop=True), inc="T")
                dY = "dY0" if par == 0 else "dY1"
                wait("vector", "T", cnt["T"])
                if h >= 2:
                    wait("vector", dY, ymm_T[h - 2][1])
                op("vector", lambda E, p=pyy, s=ystg: E.tensor_copy(s, p), inc="V")
                wait("gpsimd", "V", cnt["V"])
                op("gpsimd", lambda E, s=ystg, h=h, yoff=yoff: E.dma_start(
                    y_own[h:h + 1, yoff:yoff + CH], s), inc=dY)
                ymm_T.append((dY, cnt[dY]))
            yield
        op("vector", lambda E: E.tensor_copy(row(0), row(0)), inc="V")

    def gate_gen(j):
        gate_T = []
        for m in range(KT):
            par = m % 2
            wo_off = par * KT * 128
            pg = pb[par]
            if m >= 2:
                wait("sync", "T", gate_T[m - 2])
            else:
                wait("sync", "T", cnt["T"])
            dF = "dF0" if par == 0 else "dF1"
            dma_k("sync", dF, wfp[:, wo_off:wo_off + KT * 128],
                  hd["wg"][m * 128:(m + 1) * 128, :])
            wait("tensor", dF, cnt[dF])
            if m >= 2:
                wait("tensor", "S", sigm_S[m - 2])
            else:
                wait("tensor", "V", cnt["V"])
            for k in range(KT):
                op("tensor", lambda E, o=wo_off, k=k, p=pg: E.matmul(
                    p[:], wfp[:, o + k * 128:o + (k + 1) * 128],
                    big1[:, k * CH:(k + 1) * CH],
                    start=(k == 0), stop=(k == KT - 1)), inc=("T" if k == KT - 1 else None))
            gate_T.append(cnt["T"])
            wait("scalar", "T", cnt["T"])
            op("scalar", lambda E, p=pg, m=m: E.activation(
                big2[:, m * CH:(m + 1) * CH], p[:], AF.Sigmoid, bias=bg_r[:, m:m + 1]),
                inc="S")
            sigm_S.append(cnt["S"])
            yield

    sigm_S = []

    def quant_gen(j, round_V):
        wait("tensor", "V", cnt["V"])
        wait("scalar", "V", cnt["V"])
        wait("scalar", "dY0", cnt["dY0"])
        wait("scalar", "dY1", cnt["dY1"])
        qsq_T = []
        for k in range(KT):
            xk = big1[:, k * CH:(k + 1) * CH]
            sq = [sqv(0), sqv(1), delta(0), delta(1)][k % 4]
            if k >= 4:
                wait("scalar", "T", qsq_T[k - 4])
            op("scalar", lambda E, a=sq, b=xk: E.activation(a, b, AF.Square), inc="S")
            op("tensor", lambda E, b=xk, k=k: E.matmul(
                prow[0:1, 0:CH], onc[:], b, start=(k == 0), stop=(k == KT - 1)))
            wait("tensor", "S", cnt["S"])
            op("tensor", lambda E, a=sq, k=k: E.matmul(
                prow[0:1, CH:2 * CH], onc[:], a, start=(k == 0), stop=(k == KT - 1)),
                inc="T")
            qsq_T.append(cnt["T"])
            if k % 4 == 3:
                yield
        wait("vector", "T", cnt["T"])
        op("vector", lambda E: E.tensor_scalar(row(0), prow[0:1, 0:CH], 1.0 / D, None, ALU.mult))
        op("vector", lambda E: E.tensor_scalar(row(1), prow[0:1, CH:2 * CH], 1.0 / D, None, ALU.mult))
        op("vector", lambda E: E.tensor_tensor(row(2), row(0), row(0), ALU.mult))
        op("vector", lambda E: E.tensor_tensor(row(1), row(1), row(2), ALU.subtract))
        op("vector", lambda E: E.tensor_scalar(row(1), row(1), EPS2, None, ALU.add))
        op("vector", lambda E: E.reciprocal(row(2), row(1)), inc="V")
        wait("scalar", "V", cnt["V"])
        op("scalar", lambda E: E.activation(row(1), row(2), AF.Sqrt), inc="S")
        wait("tensor", "S", cnt["S"])
        op("tensor", lambda E: E.matmul(pb[4][:], onr[:], row(0), start=True, stop=True),
           inc="T")
        wait("vector", "T", cnt["T"])
        yield
        for k in range(KT):
            xk = big1[:, k * CH:(k + 1) * CH]
            sq = sqv(0)
            op("vector", lambda E, a=sq, b=xk: E.tensor_tensor(a, b, pb[4][:], ALU.subtract))
            if k == 0:
                op("vector", lambda E, a=sq: E.tensor_copy(mtl[:, :], a))
                op("vector", lambda E, a=sq: E.tensor_copy(mxc[:, :], a))
            else:
                op("vector", lambda E, a=sq: E.tensor_tensor(mtl[:, :], mtl[:, :], a, ALU.max))
                op("vector", lambda E, a=sq: E.tensor_tensor(mxc[:, :], mxc[:, :], a, ALU.min))
            if k % 4 == 3:
                yield
        op("vector", lambda E: E.tensor_scalar(mxc[:, :], mxc[:, :], -1.0, None, ALU.mult))
        op("vector", lambda E: E.tensor_tensor(mtl[:, :], mtl[:, :], mxc[:, :], ALU.max))
        op("vector", lambda E: E.tensor_copy(row(0), row(0)), inc="V")
        wait("tensor", "V", cnt["V"])
        wait("tensor", "G", g0_setup)
        for q in range(4):
            op("tensor", lambda E, q=q: E.transpose(
                pb[5][:, q * 128:(q + 1) * 128], mtl[:, q * 128:(q + 1) * 128], ident[:]),
                inc=("T" if q == 3 else None))
        wait("vector", "T", cnt["T"])
        for q in range(4):
            op("vector", lambda E, q=q: E.reduce_max(
                sqv(0)[:, q:q + 1], pb[5][:, q * 128:(q + 1) * 128], mybir.AxisListType.X))
        op("vector", lambda E: E.tensor_copy(row(0)[0:1, 0:1], row(0)[0:1, 0:1]), inc="V")
        wait("gpsimd", "V", cnt["V"])
        sq0 = sqv(0)
        r3 = row(3)
        for q in range(4):
            op("gpsimd", lambda E, q=q: E.dma_start(
                bass.AP(r3.tensor, r3.offset + q * 128, [[r3.ap[0][0], 1], [1, 128]]),
                bass.AP(sq0.tensor, sq0.offset + q, [[sq0.ap[0][0], 128], [1, 1]]),
                allow_slow_non_contiguous=True), inc="dQ")
        wait("vector", "dQ", cnt["dQ"])
        op("vector", lambda E: E.reciprocal(row(2), row(3)))
        op("vector", lambda E: E.tensor_scalar(row(4), row(2), 127.0, None, ALU.mult))
        op("vector", lambda E: E.tensor_tensor(row(5), row(0), row(4), ALU.mult))
        op("vector", lambda E: E.tensor_scalar(row(5), row(5), -1.0, None, ALU.mult))
        op("vector", lambda E: E.tensor_scalar(row(6), row(3), 1.0 / 127.0, None, ALU.mult))
        op("vector", lambda E: E.tensor_tensor(row(6), row(6), row(1), ALU.mult), inc="V")
        wait("tensor", "V", cnt["V"])
        op("tensor", lambda E: E.matmul(pb[4][:], onr[:], row(6), start=True, stop=True), inc="T")
        wait("vector", "T", cnt["T"])
        op("vector", lambda E: E.tensor_copy(mxc[:, :], pb[4][:]), inc="V")
        wait("tensor", "V", cnt["V"])
        op("tensor", lambda E: E.matmul(pb[4][:], onr[:], row(4), start=True, stop=True))
        op("tensor", lambda E: E.matmul(pb[5][:], onr[:], row(5), start=True, stop=True), inc="T")
        wait("vector", "T", cnt["T"])
        yield
        for k in range(KT):
            xk = big1[:, k * CH:(k + 1) * CH]
            vv, dd = sqv(0), sqv(1)
            xqs = xqb[:, k * CH:(k + 1) * CH]
            op("vector", lambda E, v=vv, x=xk: E.tensor_tensor(v, x, pb[4][:], ALU.mult))
            op("vector", lambda E, v=vv: E.tensor_tensor(v, v, pb[5][:], ALU.add))
            op("vector", lambda E, v=vv: E.tensor_copy(i32t[:, :], v))
            op("vector", lambda E, d=dd: E.tensor_copy(d, i32t[:, :]))
            op("vector", lambda E, v=vv, d=dd: E.tensor_tensor(v, v, d, ALU.subtract))
            op("vector", lambda E, v=vv: E.tensor_scalar(mtl[:, :], v, 0.5, None, ALU.is_gt))
            op("vector", lambda E, v=vv: E.tensor_scalar(v, v, -0.5, None, ALU.is_lt))
            op("vector", lambda E, v=vv: E.tensor_tensor(mtl[:, :], mtl[:, :], v, ALU.subtract))
            op("vector", lambda E, d=dd: E.tensor_tensor(d, d, mtl[:, :], ALU.add))
            op("vector", lambda E, x=xqs, d=dd: E.tensor_scalar(
                x, d, -128.0, 127.0, ALU.max, ALU.min))
            round_V.append(cnt["V"])
            if k % 2 == 1:
                yield
        op("vector", lambda E: E.tensor_copy(row(0)[0:1, 0:1], row(0)[0:1, 0:1]), inc="V")

    def yfwo_gen(j, round_V):
        ysl = y_own[:, j * CH:(j + 1) * CH]
        yf_V = []
        for m in range(KT):
            par = m % 2
            pyb = pb[2 + par]
            if m == 0:
                wait("tensor", "dY0", cnt["dY0"])
                wait("tensor", "dY1", cnt["dY1"])
                wait("tensor", "G", g0_setup)
                wait("tensor", "S", cnt["S"])
            if m >= 2:
                wait("tensor", "V", yf_V[m - 2])
            op("tensor", lambda E, m=m, p=pyb: E.matmul(
                p[:], ehb[:, m * 128:(m + 1) * 128], ysl, start=True, stop=True), inc="T")
            wait("vector", "T", cnt["T"])
            xk = big1[:, m * CH:(m + 1) * CH]
            gk = big2[:, m * CH:(m + 1) * CH]
            tv = bsq(0)
            op("vector", lambda E, x=xk, p=pyb, m=m, t=tv: E.scalar_tensor_tensor(
                t, x, dp_r[:, m:m + 1], p[:], ALU.mult, ALU.add))
            op("vector", lambda E, g=gk, t=tv: E.tensor_tensor(g, g, t, ALU.mult), inc="V")
            yf_V.append(cnt["V"])
            if m % 4 == 3:
                yield
        for _ in wo_gen(j, round_V):
            yield

    def wo_gen(j, round_V):
        wo_T = []
        acc_V = []
        for m in range(KT):
            bank = m % 4
            wo_off = (m % 2) * KT * 128
            pg = pb[bank]
            if m >= 2:
                wait("sync", "T", wo_T[m - 2])
            else:
                wait("sync", "V", cnt["V"])
                wait("sync", "T", cnt["T"])
            dF = "dF0" if m % 2 == 0 else "dF1"
            dma_k("sync", dF, wfp[:, wo_off:wo_off + KT * 128],
                  hd["wo"][m * 128:(m + 1) * 128, :])
            wait("tensor", dF, cnt[dF])
            if m >= 4:
                wait("tensor", "V", acc_V[m - 4])
            else:
                wait("tensor", "V", cnt["V"])
            for k in range(KT):
                op("tensor", lambda E, o=wo_off, k=k, p=pg: E.matmul(
                    p[:], wfp[:, o + k * 128:o + (k + 1) * 128],
                    big2[:, k * CH:(k + 1) * CH],
                    start=(k == 0), stop=(k == KT - 1)), inc=("T" if k == KT - 1 else None))
            wo_T.append(cnt["T"])
            while len(round_V) <= m:
                yield
            wait("vector", "T", wo_T[m])
            ok = big1[:, m * CH:(m + 1) * CH]
            op("vector", lambda E, o=ok, p=pg: E.tensor_copy(o, p[:]), inc="V")
            acc_V.append(cnt["V"])
            yield

    def load_x_chunk(src, j):
        wait("sync", "V", cnt["V"])
        wait("sync", "T", cnt["T"])
        wait("sync", "dC", cnt["dC"])
        for k in range(KT):
            dma_k("sync", "dX", big1[:, k * CH:(k + 1) * CH],
                  src[k * 128:(k + 1) * 128, j * CH:(j + 1) * CH])
        wait("scalar", "dX", cnt["dX"])
        wait("tensor", "dX", cnt["dX"])
        wait("vector", "dX", cnt["dX"])

    sgen = sign_gen()

    def prep_gen(j):
        load_x_chunk(hd["xpre"], j)
        yield
        layernorm_chunk(EPS1)
        yield
        ssm_chunk(j % 2)
        yield

    if PIPE_PREFIX:
        drive(prep_gen(0), bg=sgen, bg_n=2)
        for j in range(NCH):
            gens = [scan_gen(False, 0, j % 2, dbrows=(3, 4))]
            if j + 1 < NCH:
                gens.append(prep_gen(j + 1))
            drive(*gens, bg=sgen, bg_n=2)
    else:
        for j in range(NCH):
            drive(prep_gen(j))
            drive(scan_gen(False, 0, j % 2, dbrows=(3, 4)))
    op("vector", lambda E: E.tensor_scalar(st[:, :], st[:, :], msk[:, 0:1], None, ALU.mult),
       inc="V")
    drive(sgen)
    sign_done = (cnt["dO0"], cnt["dO1"])

    for j in range(NCH):
        load_x_chunk(hd["xown"], j)
        layernorm_chunk(EPS1, ring2=False)
        ssm_chunk(0)
        sigm_S.clear()
        if PIPE_SCANGATE:
            drive(scan_gen(True, j * CH, 0), gate_gen(j))
        else:
            drive(scan_gen(True, j * CH, 0))
            drive(gate_gen(j))
        round_V = []
        if PIPE_WOQUANT:
            drive(yfwo_gen(j, round_V), quant_gen(j, round_V))
        else:
            drive(yfwo_gen(j, round_V))
            drive(quant_gen(j, round_V))
        wait("sync", "dO0", sign_done[0])
        wait("sync", "dO1", sign_done[1])
        wait("gpsimd", "dO0", sign_done[0])
        wait("gpsimd", "dO1", sign_done[1])
        wait("vector", "dO0", sign_done[0])
        wait("vector", "dO1", sign_done[1])
        wait("tensor", "V", cnt["V"])
        gu_T = []
        ev_hist = []
        psplit_hist = []
        wd_last_T = []
        accD_hist = []
        xres_k = [0]
        xadd_V = []

        def xres_step():
            k = xres_k[0]
            if k >= KT:
                return
            xres_k[0] += 1
            stg = mtl[:, :] if k % 2 == 0 else gt2[:, :]
            if k >= 2:
                wait("scalar", "V", xadd_V[k - 2])
            else:
                wait("scalar", "V", cnt["V"])
            dCp = "dC0" if k % 2 == 0 else "dC1"
            op("scalar", lambda E, k=k, s=stg: E.dma_start(
                s, hd["xown"][k * 128:(k + 1) * 128, j * CH:(j + 1) * CH]), inc=dCp)
            wait("vector", dCp, cnt[dCp])
            ok = big1[:, k * CH:(k + 1) * CH]
            op("vector", lambda E, o=ok, s=stg: E.tensor_tensor(o, o, s, ALU.add), inc="V")
            xadd_V.append(cnt["V"])

        def gubuf(g):
            return scrA if g % 2 == 0 else big2

        def gu_part(g):
            buf = gubuf(g)
            for mi in range(GMT):
                hm = g * GMT + mi
                par2 = hm % 2
                og = (2 * par2) * KT * 128
                ou = (2 * par2 + 1) * KT * 128
                if len(gu_T) >= 2:
                    wait("sync", "T", gu_T[-2])
                dU = "dU0" if par2 == 0 else "dU1"
                dma_k("sync", dU, wgu[:, og:og + KT * 128],
                      hd["sg_s"][hm * 128:(hm + 1) * 128, :])
                dma_k("sync", dU, wgu[:, ou:ou + KT * 128],
                      hd["su_s"][hm * 128:(hm + 1) * 128, :])
                pgg, puu = pb[mi % 2], pb[2 + mi % 2]
                wait("tensor", dU, cnt[dU])
                if mi >= 2:
                    wait("tensor", "V", ev_hist[-2])
                elif g > 0 and ev_hist:
                    wait("tensor", "V", ev_hist[-1])
                for k in range(KT):
                    op("tensor", lambda E, o=og, k=k, p=pgg: E.matmul(
                        p[:], wgu[:, o + k * 128:o + (k + 1) * 128],
                        xqb[:, k * CH:(k + 1) * CH],
                        start=(k == 0), stop=(k == KT - 1)))
                for k in range(KT):
                    op("tensor", lambda E, o=ou, k=k, p=puu: E.matmul(
                        p[:], wgu[:, o + k * 128:o + (k + 1) * 128],
                        xqb[:, k * CH:(k + 1) * CH],
                        start=(k == 0), stop=(k == KT - 1)),
                        inc=("T" if k == KT - 1 else None))
                gu_T.append(cnt["T"])
                wait("vector", "T", cnt["T"])
                gsl = buf[:, mi * CH:(mi + 1) * CH]
                usl = buf[:, (GMT + mi) * CH:(GMT + mi + 1) * CH]
                op("vector", lambda E, a=gsl, p=pgg: E.tensor_tensor(a, p[:], mxc[:, :], ALU.mult))
                op("vector", lambda E, a=usl, p=puu: E.tensor_copy(a, p[:]), inc="V")
                ev_hist.append(cnt["V"])
                if mi % 2 == 1:
                    xres_step()

        def pchain(g):
            buf = gubuf(g)
            gall = buf[:, 0:GMT * CH]
            uall = buf[:, GMT * CH:2 * GMT * CH]
            wait("scalar", "V", ev_hist[-1])
            op("scalar", lambda E, a=gall: E.activation(a, a, AF.Silu), inc="S")
            wait("vector", "S", cnt["S"])
            if len(wd_last_T) >= 1:
                wait("vector", "T", wd_last_T[-1])
            op("vector", lambda E, a=gall, b=uall: E.tensor_tensor(a, a, b, ALU.mult))
            op("vector", lambda E, a=gall: E.tensor_tensor(
                a.rearrange("p (t c) -> p t c", c=CH),
                a.rearrange("p (t c) -> p t c", c=CH),
                mxc[:, :].unsqueeze(1).broadcast_to([128, GMT, CH]), ALU.mult))
            op("vector", lambda E, a=gall: E.tensor_copy(p_hi[:, :], a))
            op("vector", lambda E, a=gall: E.tensor_tensor(p_lo[:, :], a, p_hi[:, :], ALU.subtract),
               inc="V")
            psplit_hist.append(cnt["V"])

        def wd_part(g):
            wd_T = []
            for m in range(KT):
                par3 = m % 2
                od = par3 * GMT * 128
                if m >= 2:
                    wait("gpsimd", "T", wd_T[m - 2])
                elif wd_last_T:
                    wait("gpsimd", "T", wd_last_T[-1])
                dma_k("gpsimd", "dW0" if par3 == 0 else "dW1",
                      wdd[:, od:od + GMT * 128],
                      hd["sd_s"][m * 128:(m + 1) * 128, g * KG:(g + 1) * KG])
                pdd = pb[4 + par3]
                wait("tensor", "dW0" if par3 == 0 else "dW1",
                     cnt["dW0" if par3 == 0 else "dW1"])
                if m == 0:
                    wait("tensor", "V", psplit_hist[-1])
                    if len(accD_hist) >= 1:
                        wait("tensor", "V", accD_hist[-1])
                elif m == 1 and len(accD_hist) >= 1:
                    wait("tensor", "V", accD_hist[-1])
                elif m >= 2:
                    wait("tensor", "V", accD_hist[-(KT - m + 2)] if False else accD_now[m - 2])
                for kk in range(GMT):
                    op("tensor", lambda E, o=od, kk=kk, p=pdd: E.matmul(
                        p[:], wdd[:, o + kk * 128:o + (kk + 1) * 128],
                        p_hi[:, kk * CH:(kk + 1) * CH],
                        start=(kk == 0), stop=False))
                for kk in range(GMT):
                    op("tensor", lambda E, o=od, kk=kk, p=pdd: E.matmul(
                        p[:], wdd[:, o + kk * 128:o + (kk + 1) * 128],
                        p_lo[:, kk * CH:(kk + 1) * CH],
                        start=False, stop=(kk == GMT - 1)),
                        inc=("T" if kk == GMT - 1 else None))
                wd_T.append(cnt["T"])
                wait("vector", "T", cnt["T"])
                ok = big1[:, m * CH:(m + 1) * CH]
                op("vector", lambda E, o=ok, p=pdd: E.tensor_tensor(o, o, p[:], ALU.add), inc="V")
                accD_now.append(cnt["V"])
            wd_last_T.append(wd_T[-1])
            accD_hist.append(accD_now[-1])

        if PIPE_FFN:
            gu_part(0)
            for g in range(1, NGRP):
                pchain(g - 1)
                accD_now = []
                gu_part(g)
                wd_part(g - 1)
            pchain(NGRP - 1)
            accD_now = []
            wd_part(NGRP - 1)
        else:
            for g in range(NGRP):
                gu_part(g)
                pchain(g)
                accD_now = []
                wd_part(g)
        while xres_k[0] < KT:
            xres_step()
        wait("scalar", "V", cnt["V"])
        for k in range(KT):
            dma("scalar", hd["out_t"][k * 128:(k + 1) * 128, j * CH:(j + 1) * CH],
                big1[:, k * CH:(k + 1) * CH])

    wait("gpsimd", "dC", cnt["dC"])
    op("gpsimd", lambda E: E.memset(st[0:1, 0:1], 0.0), inc="G")


_NC_CACHE = {}


def kernel(L0_attn_x, L0_attn_Wx, L0_attn_Wg, L0_attn_bg, L0_attn_Wo,
           L0_attn_A, L0_attn_Dp, L0_ffn_W_g, L0_ffn_W_u, L0_ffn_W_d):
    x = np.ascontiguousarray(np.asarray(L0_attn_x, dtype=np.float32))
    wx = np.ascontiguousarray(np.asarray(L0_attn_Wx, dtype=np.float32))
    wg = np.ascontiguousarray(np.asarray(L0_attn_Wg, dtype=np.float32))
    bg = np.ascontiguousarray(np.asarray(L0_attn_bg, dtype=np.float32))
    wo = np.ascontiguousarray(np.asarray(L0_attn_Wo, dtype=np.float32))
    a = np.ascontiguousarray(np.asarray(L0_attn_A, dtype=np.float32))
    dp = np.ascontiguousarray(np.asarray(L0_attn_Dp, dtype=np.float32))
    w_g = np.ascontiguousarray(np.asarray(L0_ffn_W_g, dtype=np.float32))
    w_u = np.ascontiguousarray(np.asarray(L0_ffn_W_u, dtype=np.float32))
    w_d = np.ascontiguousarray(np.asarray(L0_ffn_W_d, dtype=np.float32))

    xT = np.ascontiguousarray(x.transpose(0, 2, 1))
    wg_st = np.ascontiguousarray(
        wg.reshape(KT, 128, KT, 128).transpose(2, 1, 0, 3).reshape(D, D))
    wo_st = np.ascontiguousarray(
        wo.reshape(KT, 128, KT, 128).transpose(2, 1, 0, 3).reshape(D, D))
    wgtT = np.ascontiguousarray(w_g.T)
    wutT = np.ascontiguousarray(w_u.T)
    wdtT = np.ascontiguousarray(w_d.T)
    a_tT = np.ascontiguousarray(a.T)
    dp_til = np.ascontiguousarray(np.repeat(dp, HD).reshape(KT, 128).T)
    bg_til = np.ascontiguousarray(bg.reshape(KT, 128).T)
    zeros_pre = np.zeros((D, 2048), dtype=np.float32)

    if "nc" not in _NC_CACHE:
        _NC_CACHE["nc"] = build_nc()
    nc = _NC_CACHE["nc"]

    in_maps = []
    for c in range(8):
        b, half = c // 2, c % 2
        xpre = zeros_pre if half == 0 else np.ascontiguousarray(xT[b, :, 0:2048])
        xown = np.ascontiguousarray(xT[b, :, half * 2048:(half + 1) * 2048])
        m = np.full((128, 1), 0.0 if half == 0 else 1.0, dtype=np.float32)
        in_maps.append({
            "xpre": xpre, "xown": xown, "smask": m,
            "wx": wx, "wg": wg_st, "wo": wo_st,
            "wgt": wgtT, "wut": wutT, "wdt": wdtT,
            "a_t": a_tT, "dp_t": dp_til, "bg_t": bg_til,
        })
    res = run_bass_kernel_spmd(nc, in_maps, core_ids=list(range(8)))
    out = np.empty((B, L, D), dtype=np.float32)
    for c in range(8):
        b, half = c // 2, c % 2
        out[b, half * 2048:(half + 1) * 2048, :] = res.results[c]["out_t"].T
    return out
